# version 23
# baseline (speedup 1.0000x reference)
import contextlib
import numpy as np

import concourse.bass as bass
import concourse.mybir as mybir
from concourse.bass_utils import run_bass_kernel_spmd

F32 = mybir.dt.float32
BF16 = mybir.dt.bfloat16
AF = mybir.ActivationFunctionType
ALU = mybir.AluOpType
AX = mybir.AxisListType

RND = 12582912.0
EPS = 1e-5
D = 1024
N = 4096
TJ = 2048
DI = 2816
NCORES = 8
DC = 8
NTT = 8
NJT = 8
NIC = 32
NICC = 22
KBB = 1024


class _C:
    pass


class Sem:
    def __init__(self, es, nc, name):
        self.h = es.enter_context(nc.semaphore(name))
        self.n = 0

    def inc(self, instr, k=1):
        instr.then_inc(self.h, k)
        self.n += k
        return instr


ENGS = ("tensor", "scalar", "vector", "gpsimd", "sync")


class Sched:

    def __init__(self, c):
        self.c = c
        self.items = {e: [] for e in ENGS}
        self.seg = "pre"
        self.replayed = {e: 0 for e in ENGS}
        self.emitted_pos = {e: 0 for e in ENGS}

    def add(self, eng, emit, relax=None, extra=None):
        c = self.c
        waits = {}
        for e in ENGS:
            if e == eng:
                continue
            tgt = c.chain[e].n if relax is None or e not in relax else relax[e]
            if tgt > c.chain_seen[eng][e]:
                waits[e] = tgt
        self.items[eng].append((dict(waits), emit, extra or [], self.seg))
        for e, v in waits.items():
            c.chain_seen[eng][e] = v
        c.chain[eng].n += 16 if getattr(emit, "is_dma", False) else 1
        return c.chain[eng].n

    def mark(self, eng):
        return self.c.chain[eng].n

    def replay(self, eng, e, seg=None):
        c = self.c
        start = self.replayed[eng]
        idx = start
        own = self.emitted_pos[eng]
        for waits, emit, extra, iseg in self.items[eng][start:]:
            if seg is not None and iseg != seg:
                break
            e.wait_ge(c.chain[eng].h, c.chain[eng].n if False else own)
            for oe, v in waits.items():
                e.wait_ge(c.chain[oe].h, v)
            for sname, v in extra:
                h = sname.h if isinstance(sname, Sem) else c.s[sname].h
                e.wait_ge(h, v)
            ins = emit(e)
            k = 16 if getattr(emit, "is_dma", False) else 1
            ins.then_inc(c.chain[eng].h, k)
            own += k
            idx += 1
        self.replayed[eng] = idx
        self.emitted_pos[eng] = own


def dma_step(fn):
    fn.is_dma = True
    return fn



def build_program(stage=3, taps=False):
    nc = bass.Bass("TRN2", target_bir_lowering=False, debug=False,
                   detect_race_conditions=False)
    c = _C()
    c.nc = nc
    c.stage = stage
    c.taps = taps
    es = contextlib.ExitStack()

    c.xT = nc.dram_tensor("xT", [D, N], F32, kind="ExternalInput").ap()
    c.wq_d = nc.dram_tensor("wq", [D, D], F32, kind="ExternalInput").ap()
    c.wk_d = nc.dram_tensor("wk", [D, D], F32, kind="ExternalInput").ap()
    c.wv_d = nc.dram_tensor("wv", [D, D], F32, kind="ExternalInput").ap()
    c.wgT = nc.dram_tensor("wgT", [D, DI], F32, kind="ExternalInput").ap()
    c.wuT = nc.dram_tensor("wuT", [D, DI], F32, kind="ExternalInput").ap()
    c.wdT = nc.dram_tensor("wdT", [DI, D], F32, kind="ExternalInput").ap()
    c.clip = nc.dram_tensor("clip", [1, 2], F32, kind="ExternalInput").ap()

    sk = "ExternalOutput" if taps else "Internal"
    c.v_sp = nc.dram_tensor("v_sp", [N, D], BF16, kind=sk).ap()
    c.x2_sp = nc.dram_tensor("x2_sp", [DC, 128, TJ], F32, kind=sk).ap()
    c.h_sp = nc.dram_tensor("h_sp", [NICC, 128, TJ], F32,
                            kind=sk if stage >= 3 else "Internal").ap()
    c.cc1i = nc.dram_tensor("cc1i", [1, 1], F32, kind="Internal").ap()
    c.cc1o = nc.dram_tensor("cc1o", [1, 1], F32, kind="Internal").ap()
    c.cc2i = nc.dram_tensor("cc2i", [1, 1], F32, kind="Internal").ap()
    c.cc2o = nc.dram_tensor("cc2o", [1, 1], F32, kind="Internal").ap()
    if stage >= 3:
        c.outT = nc.dram_tensor("outT", [D, TJ], F32, kind="ExternalOutput").ap()
    if taps:
        c.dbg_q = nc.dram_tensor("dbg_q", [128, DC * N], BF16, kind="ExternalOutput").ap()
        c.dbg_k = nc.dram_tensor("dbg_k", [128, DC * TJ], BF16, kind="ExternalOutput").ap()
        c.dbg_mup = nc.dram_tensor("dbg_mup", [1, N], BF16, kind="ExternalOutput").ap()
        c.dbg_rows = nc.dram_tensor("dbg_rows", [1, 4 * TJ], F32, kind="ExternalOutput").ap()
        c.dbg_vst = nc.dram_tensor("dbg_vst", [128, D], BF16, kind="ExternalOutput").ap()
        if stage >= 3:
            c.dbg_qx = nc.dram_tensor("dbg_qx", [128, DC * TJ], BF16, kind="ExternalOutput").ap()
            c.dbg_sc = nc.dram_tensor("dbg_sc", [128, 80], F32, kind="ExternalOutput").ap()

    c.xT_r = c.xT.rearrange("(c p) t -> c p t", p=128)
    c.wq_r = c.wq_d.rearrange("(c p) e -> c p e", p=128)
    c.wk_r = c.wk_d.rearrange("(c p) e -> c p e", p=128)
    c.wv_r = c.wv_d.rearrange("(c p) e -> c p e", p=128)
    c.wgT_r = c.wgT.rearrange("(c p) i -> c p i", p=128)
    c.wuT_r = c.wuT.rearrange("(c p) i -> c p i", p=128)
    c.wdT_r = c.wdT.rearrange("(c p) h -> c p h", p=128)

    base = (nc.sbuf_base + 63) & ~63

    def sb(name, shape, dtype, off_kb):
        return nc.alloc_sbuf_tensor_at(
            name, shape, dtype, offset=base + int(off_kb * KBB)).ap()

    c.ident = sb("ident", [128, 128], F32, 0.0)
    c.ones1 = sb("ones1", [1, 128], F32, 0.5)
    c.onesc = sb("onesc", [128, 1], BF16, 1.0)
    c.cexp_s = sb("cexp_s", [128, 1], F32, 1.0625)
    c.cexp_b = sb("cexp_b", [128, 1], F32, 1.125)
    c.clip_sb = sb("clip_sb", [1, 2], F32, 1.1875)
    c.smalls = sb("smalls", [128, 16], F32, 1.25)
    c.SM_SCALEX, c.SM_SCALED, c.SM_INVSD, c.SM_HMAX, c.SM_LNR = 0, 1, 2, 3, 4
    c.SM_MC = 8
    c.rsg = sb("rsg", [128, 22], F32, 1.3125)
    c.rsu = sb("rsu", [128, 22], F32, 1.40625)
    c.rsd = sb("rsd", [128, 8], F32, 1.5)
    c.invw_col = sb("invw_col", [128, 22], F32, 1.53125)
    c.scal1 = sb("scal1", [1, 8], F32, 1.625)
    c.hmax_col = sb("hmax_col", [128, 1], F32, 1.6875)
    c.ones1b = sb("ones1b", [1, 128], BF16, 1.75)
    c.r_row = sb("r_row", [1, 256], F32, 2.0)
    c.rinv_row = sb("rinv_row", [1, 256], F32, 3.0)
    c.mup_row = sb("mup_row", [1, N], BF16, 4.0)
    c.var2_row = sb("var2_row", [1, TJ], F32, 4.0)
    c.b_row = c.var2_row
    c.scr_mu = sb("scr_mu", [1, 512], F32, 12.0)
    c.scr_var = sb("scr_var", [1, 512], F32, 14.0)
    c.scr_tmp = sb("scr_tmp", [1, 512], F32, 16.0)
    c.mu2_row = sb("mu2_row", [1, TJ], F32, 12.0)
    c.mup_bc = sb("mup_bc", [128, 512], F32, 18.0)
    c.m_row = sb("m_row", [1, TJ], F32, 20.0)
    c.a_row = c.m_row
    c.winv_row = sb("winv_row", [1, DI], F32, 28.0)
    c.invsig_bc = sb("invsig_bc", [128, 512], F32, 24.0)
    c.vstage = [sb(f"vstage{i}", [128, D], BF16, 39.0 + 2 * i) for i in range(2)]
    c.Mrow_s = sb("Mrow_s", [1, 256], F32, 39.0)
    c.mrow_s = sb("mrow_s", [1, 256], F32, 40.0)
    c.trow_s = sb("trow_s", [1, 256], F32, 41.0)
    c.hmcols = sb("hmcols", [128, NICC * 4 + 8], F32, 42.0)
    c.q_sb = sb("q_sb", [128, DC * N], BF16, 43.0)
    c.qwg = sb("qwg", [128, DC * DI], BF16, 43.0)
    c.qh1 = sb("qh1", [128, 11 * TJ], BF16, 43.0)
    c.qwu = sb("qwu", [128, DC * DI], BF16, 87.0)
    c.qwd = sb("qwd", [128, NICC * D], BF16, 87.0)
    c.k_sb = sb("k_sb", [128, DC * TJ], BF16, 107.0)
    c.qx = sb("qx", [128, DC * TJ], BF16, 131.0)
    c.qh2 = sb("qh2", [128, 11 * TJ], BF16, 131.0)
    c.wq_sb = sb("wq_sb", [128, DC * D], BF16, 139.0)
    c.wk_sb = sb("wk_sb", [128, DC * D], BF16, 155.0)
    c.wv_sb = sb("wv_sb", [128, DC * D], BF16, 171.0)
    c.bufx = [sb(f"bufx{i}", [128, DC * 512], BF16, 187.0 + 8 * i) for i in range(2)]
    c.bufx.append(sb("bufx2", [128, DC * 512], BF16, 30.0))
    c.xsqA = sb("xsqA", [128, 4 * 512], BF16, 203.0)
    c.xsqB = sb("xsqB", [128, 4 * 512], BF16, 26.0)
    c.E = sb("E", [128, NIC * 256], BF16, 139.0)
    c.x2stage = sb("x2stage", [128, TJ], F32, 155.0)
    c.x2b = sb("x2b", [128, TJ], BF16, 163.0)
    c.vbuf = [sb(f"vbuf{i}", [128, D], BF16, 167.0 + 2 * i) for i in range(4)]
    c.xresb = sb("xresb", [128, TJ], F32, 30.0)
    c.rinv_bc = sb("rinv_bc", [128, 256], F32, 177.0)
    c.Mt = sb("Mt", [128, 256], F32, 178.0)
    c.mt = sb("mt", [128, 256], F32, 179.0)
    c.m1g = sb("m1g", [128, DI], F32, 180.0)
    c.m1u = sb("m1u", [128, DI], F32, 191.0)
    c.wp1 = [sb(f"wp1{i}", [128, 352], F32, 202.0 + 1.375 * i)
             for i in range(4)]
    c.binv = sb("binv", [128, DI], F32, 163.0)
    c.a_bc = sb("a_bc", [128, 1024], F32, 175.0)
    c.b_bc = sb("b_bc", [128, 1024], F32, 179.0)
    c.x2re = [sb(f"x2re{i}", [128, 1024], F32, 183.0 + 4 * i) for i in range(2)]
    c.wp2h = [sb(f"wp2{i}", [128, DI // 4], F32, 191.0 + 2.75 * i)
              for i in range(4)]
    c.htile = [sb(f"htile{i}", [128, 512], F32, 183.0 + 2 * i) for i in range(2)]
    c.sg = sb("sg", [128, 512], F32, 187.0)
    c.su = sb("su", [128, 512], F32, 189.0)
    c.wdbuf = [sb(f"wdbuf{i}", [128, D], F32, 163.0 + 4 * i) for i in range(3)]
    c.m1d = sb("m1d", [128, D], F32, 171.0)
    c.binvd = sb("binvd", [128, D], F32, 175.0)
    c.hre = [sb(f"hre{i}", [128, TJ // 2], F32, 175.0 + 4 * i)
             for i in range(4)]
    c.x2re2 = [sb("x2re2a", [128, TJ], F32, 175.0),
               sb("x2re2b", [128, TJ], F32, 183.0)]
    c.ostage = [sb(f"ostage{i}", [128, TJ], F32, 191.0 + 8 * i) for i in range(2)]

    def qh_ap(icc):
        return (c.qh1[:, icc * TJ:(icc + 1) * TJ] if icc < 11
                else c.qh2[:, (icc - 11) * TJ:(icc - 10) * TJ])
    c.qh_ap = qh_ap

    c.pb = [nc.alloc_psum_tensor(f"pb{i}", [128, 512], F32) for i in range(8)]

    names = [
        "gi", "dg", "ssq", "tstat", "tsx", "vstat", "sinv", "vmup", "tbc",
        "vbc", "vxp", "tqg", "sev", "tqkv",
        "tS", "sE", "tAV", "vOev", "trr", "vrv", "trt", "vrt", "tb7", "vb7",
        "vxr", "dx2", "sx2a", "tst2a", "sx2b", "tst2b", "vst2",
        "tmt", "vmt", "vmc", "vmrow", "vwp",
        "tgu", "sgu", "vh", "vwd1",
        "tdn", "sdn", "vdn", "dout2",
    ]
    c.s = {n: Sem(es, nc, n) for n in names}
    c.dxa = [Sem(es, nc, f"dxa{i}") for i in range(3)]
    c.dvb = [Sem(es, nc, f"dvb{i}") for i in range(4)]
    c.dvsp = [Sem(es, nc, f"dvsp{i}") for i in range(2)]
    c.dxr = [Sem(es, nc, f"dxr{i}") for i in range(2)]
    c.dhsp = [Sem(es, nc, f"dhsp{i}") for i in range(2)]
    c.dwp = [Sem(es, nc, f"dwp{i}") for i in range(4)]
    c.dp2 = [Sem(es, nc, f"dp2{i}") for i in range(4)]
    c.dwd2 = [Sem(es, nc, f"dwd2{i}") for i in range(3)]
    c.dqh = [Sem(es, nc, f"dqh{i}") for i in range(4)]
    c.dwd1 = [Sem(es, nc, f"dwd1{i}") for i in range(2)]
    c.dx2b = [Sem(es, nc, f"dx2b{i}") for i in range(2)]
    c.dout = [Sem(es, nc, f"dout{i}") for i in range(2)]
    c.chain = {e: Sem(es, nc, f"ch_{e}") for e in ENGS}
    c.chain_seen = {e: {o: 0 for o in ENGS} for e in ENGS}
    c.sched = Sched(c)

    c.mk = {}

    _schedule_f(c)

    with es:
        with nc.Block() as block:
            _emit_gpsimd(c, block)
            _emit_sync(c, block)
            _emit_tensor(c, block)
            _emit_scalar(c, block)
            _emit_vector(c, block)
    return nc



def _a_groups(tt):
    gs = []
    for ec in range(8):
        gs.append(("q", ec, tt))
    if tt < 4:
        for ec in range(8):
            gs.append(("k", ec, tt))
    for tc in range(tt * 4, tt * 4 + 4):
        for eh in range(2):
            gs.append(("v", tc, eh))
    return gs


def _a_group_count(tt):
    return sum(len(_a_groups(t)) for t in range(tt))


A_GROUPS_TOTAL = sum(len(_a_groups(t)) for t in range(NTT))


def _vev_count(tc):
    tt = tc // 4
    for gi_, g in enumerate(_a_groups(tt)):
        if g[0] == "v" and g[1] == tc and g[2] == 1:
            return _a_group_count(tt) + gi_ + 1
    raise AssertionError



def _schedule_f(c):
    if c.stage < 3:
        return
    sd = c.sched
    s = c.s

    def v_rows1(v):
        v.tensor_scalar_add(c.var2_row[0:1, :], c.var2_row[0:1, :], EPS)
        v.reciprocal(c.var2_row[0:1, :], c.var2_row[0:1, :])
        v.tensor_mul(c.m_row[0:1, :], c.m_row[0:1, :], c.m_row[0:1, :])
        v.tensor_mul(c.m_row[0:1, :], c.m_row[0:1, :], c.var2_row[0:1, :])
        return v.reduce_max(c.smalls[0:1, c.SM_LNR:c.SM_LNR + 1],
                            c.m_row[0:1, :], axis=AX.X)
    sd.add("vector", v_rows1, extra=[("vst2", 2 * NJT), ("vmrow", NJT)])

    @dma_step
    def y_ar1a(sy):
        return sy.dma_start(out=c.cc1i[:, :],
                            in_=c.smalls[0:1, c.SM_LNR:c.SM_LNR + 1])
    sd.add("sync", y_ar1a)

    gpos1 = sd.mark("gpsimd")

    def g_ar1b(g):
        return g.collective_compute(
            "AllReduce", ALU.max, replica_groups=[list(range(NCORES))],
            ins=[c.cc1i], outs=[c.cc1o])
    sd.add("gpsimd", g_ar1b)
    RLX1 = {"gpsimd": gpos1}

    for wi in range(2):
        m1 = (c.m1g, c.m1u)[wi]
        rs = (c.rsg, c.rsu)[wi]
        for bt in range(6):
            nt = 4 if bt < 5 else 2

            def t_rx(te, m1=m1, bt=bt, nt=nt):
                for q in range(nt):
                    ins = te.transpose(c.pb[6][:, q * 128:(q + 1) * 128],
                                       m1[:, (bt * 4 + q) * 128:(bt * 4 + q + 1) * 128],
                                       c.ident[:, :])
                return ins
            sd.add("tensor", t_rx, relax=RLX1, extra=[("vwp", 64 * (wi + 1))])

            def v_rx(v, rs=rs, bt=bt, nt=nt):
                for q in range(nt):
                    icc = bt * 4 + q
                    ins = v.reduce_max(rs[:, icc:icc + 1],
                                       c.pb[6][:, q * 128:(q + 1) * 128], axis=AX.X)
                return ins
            sd.add("vector", v_rx, relax=RLX1)

    c.mk["rx_done_t"] = sd.mark("tensor")
    p2v = []
    for wi in range(2):
        rs = (c.rsg, c.rsu)[wi]

        sd.add("vector", lambda v, rs=rs: v.tensor_scalar_mul(
            c.invw_col[:, 0:22], rs[:, 0:22], 1.0 / 127.0), relax=RLX1)
        sd.add("vector", lambda v: v.reciprocal(c.invw_col[:, 0:22],
                                                c.invw_col[:, 0:22]), relax=RLX1)

        for bt in range(6):
            nt = 4 if bt < 5 else 2

            def t_tiny(te, bt=bt, nt=nt):
                for q in range(nt):
                    ins = te.transpose(c.pb[6][0:1, q * 128:(q + 1) * 128],
                                       c.invw_col[:, bt * 4 + q:bt * 4 + q + 1],
                                       c.ident[:, :])
                return ins
            sd.add("tensor", t_tiny, relax=RLX1)

            def v_tiny(v, bt=bt, nt=nt):
                for q in range(nt):
                    icc = bt * 4 + q
                    ins = v.tensor_copy(c.winv_row[0:1, icc * 128:(icc + 1) * 128],
                                        c.pb[6][0:1, q * 128:(q + 1) * 128])
                return ins
            sd.add("vector", v_tiny, relax=RLX1)

        for bt in range(0, 6, 2):
            nn2 = 512 if bt + 1 < 5 else 256

            def t_binv(te, bt=bt, nn2=nn2):
                te.matmul(c.pb[6][:, 0:512], c.ones1[:, :],
                          c.winv_row[0:1, bt * 512:(bt + 1) * 512],
                          start=True, stop=True)
                return te.matmul(c.pb[7][:, 0:nn2], c.ones1[:, :],
                                 c.winv_row[0:1, (bt + 1) * 512:(bt + 1) * 512 + nn2],
                                 start=True, stop=True)
            sd.add("tensor", t_binv, relax=RLX1)

            def v_binv(v, bt=bt, nn2=nn2):
                v.tensor_copy(c.binv[:, bt * 512:(bt + 1) * 512],
                              c.pb[6][:, 0:512])
                return v.tensor_copy(
                    c.binv[:, (bt + 1) * 512:(bt + 1) * 512 + nn2],
                    c.pb[7][:, 0:nn2])
            sd.add("vector", v_binv, relax=RLX1)

        qw = (c.qwg, c.qwu)[wi]
        for hh in range(4 * DC):
            hc, hf = divmod(hh, 4)
            n = len(p2v)

            def v_p2(v, qw=qw, hc=hc, hf=hf, b=n % 4):
                wb = c.wp2h[b]
                v.tensor_mul(wb[:, :], wb[:, :],
                             c.binv[:, hf * 704:(hf + 1) * 704])
                return v.tensor_scalar(
                    qw[:, hc * DI + hf * 704:hc * DI + (hf + 1) * 704],
                    wb[:, :], RND, RND, ALU.add, ALU.subtract)
            p2v.append(sd.add("vector", v_p2, relax=RLX1,
                              extra=[(c.dp2[n % 4], 16 * (n // 4 + 1))]))
        if wi == 0:
            c.mk["qwg_done_v"] = sd.mark("vector")
    c.mk["qwu_done_v"] = sd.mark("vector")
    c.p2_v_pos = p2v

    sd.seg = "pre2"

    @dma_step
    def y_ar1c(sy):
        return sy.dma_start(out=c.smalls[0:1, c.SM_LNR:c.SM_LNR + 1],
                            in_=c.cc1o[:, :])
    sd.add("sync", y_ar1c)

    def s_post1(sc):
        sc.activation(c.scal1[0:1, 0:1],
                      c.smalls[0:1, c.SM_LNR:c.SM_LNR + 1], AF.Sqrt)
        return sc.activation(c.var2_row[0:1, :], c.var2_row[0:1, :], AF.Sqrt)
    sd.add("scalar", s_post1)

    def s_post2(sc):
        sc.activation(c.scal1[0:1, 2:3], c.clip_sb[0:1, 0:1], AF.Sigmoid)
        return sc.activation(c.scal1[0:1, 3:4], c.clip_sb[0:1, 1:2], AF.Sigmoid)
    sd.add("scalar", s_post2)

    sd.add("vector", lambda v: v.tensor_mul(c.scal1[0:1, 0:1], c.scal1[0:1, 0:1],
                                            c.scal1[0:1, 2:3]))
    sd.add("vector", lambda v: v.tensor_scalar_mul(c.scal1[0:1, 0:1],
                                                   c.scal1[0:1, 0:1], 1.0 / 127.0))
    sd.add("vector", lambda v: v.reciprocal(c.scal1[0:1, 1:2], c.scal1[0:1, 0:1]))
    sd.add("vector", lambda v: v.tensor_scalar(c.a_row[0:1, :], c.var2_row[0:1, :],
                                               c.scal1[0:1, 1:2], None, ALU.mult))
    sd.add("vector", lambda v: v.tensor_mul(c.b_row[0:1, :], c.mu2_row[0:1, :],
                                            c.a_row[0:1, :]))

    def t_sxcol(te):
        return te.matmul(c.pb[6][:, 0:1], c.ones1[:, :], c.scal1[0:1, 0:1],
                         start=True, stop=True)
    sd.add("tensor", t_sxcol)

    def v_sxcol(v):
        return v.tensor_copy(c.smalls[:, c.SM_SCALEX:c.SM_SCALEX + 1],
                             c.pb[6][:, 0:1])
    sd.add("vector", v_sxcol)

    for rs in (c.rsg, c.rsu):
        sd.add("vector", lambda v, rs=rs: v.tensor_scalar_mul(
            rs[:, 0:22], rs[:, 0:22], 1.0 / 127.0))
        sd.add("vector", lambda v, rs=rs: v.tensor_scalar(
            rs[:, 0:22], rs[:, 0:22],
            c.smalls[:, c.SM_SCALEX:c.SM_SCALEX + 1], None, ALU.mult))

    qxv = []
    sqx = []
    for th in range(2):
        for k in range(2):
            row = (c.a_row, c.b_row)[k]
            dst = (c.a_bc, c.b_bc)[k]

            def t_ab(te, row=row, th=th):
                te.matmul(c.pb[6][:, 0:512], c.ones1[:, :],
                          row[0:1, th * 1024:th * 1024 + 512],
                          start=True, stop=True)
                return te.matmul(c.pb[7][:, 0:512], c.ones1[:, :],
                                 row[0:1, th * 1024 + 512:th * 1024 + 1024],
                                 start=True, stop=True)
            sd.add("tensor", t_ab)

            def v_ab(v, dst=dst):
                v.tensor_copy(dst[:, 0:512], c.pb[6][:, 0:512])
                return v.tensor_copy(dst[:, 512:1024], c.pb[7][:, 0:512])
            sd.add("vector", v_ab)
        for hc in range(DC):
            n = len(qxv)
            rlx = {}
            rlx2 = {}
            if n >= 2:
                rlx["vector"] = qxv[-2]
                rlx["scalar"] = sqx[-2]
                rlx2["scalar"] = sqx[-2]

            @dma_step
            def y_qx(sy, th=th, hc=hc):
                return sy.dma_start(out=c.x2re[hc % 2][:, :],
                                    in_=c.x2_sp[hc, :, th * 1024:(th + 1) * 1024])
            sd.add("sync", y_qx, relax=rlx, extra=[("dx2", 16 * DC * NJT)])

            def v_qx1(v, hc=hc):
                xb = c.x2re[hc % 2]
                v.tensor_mul(xb[:, :], xb[:, :], c.a_bc[:, :])
                return v.tensor_sub(xb[:, :], xb[:, :], c.b_bc[:, :])
            sd.add("vector", v_qx1, relax=rlx2)

            def s_qx(sc, hc=hc):
                xb = c.x2re[hc % 2]
                sc.activation(xb[:, :], xb[:, :], AF.Identity,
                              bias=c.smalls[:, 12:13])
                return sc.activation(xb[:, :], xb[:, :], AF.Identity,
                                     bias=c.smalls[:, 13:14])
            sqx.append(sd.add("scalar", s_qx))

            def v_qx2(v, th=th, hc=hc):
                return v.tensor_scalar(
                    c.qx[:, hc * TJ + th * 1024:hc * TJ + (th + 1) * 1024],
                    c.x2re[hc % 2][:, :], 127.0, -128.0, ALU.min, ALU.max)
            qxv.append(sd.add("vector", v_qx2))
    c.mk["qx_done_v"] = sd.mark("vector")

    sd.seg = "post"
    def t_hmaxT(te):
        return te.transpose(c.pb[6][0:1, 0:128], c.hmax_col[:, :], c.ident[:, :])
    sd.add("tensor", t_hmaxT, extra=[("vh", NICC * 4 + 2)])

    def v_hmax(v):
        return v.reduce_max(c.smalls[0:1, c.SM_HMAX:c.SM_HMAX + 1],
                            c.pb[6][0:1, 0:128], axis=AX.X)
    sd.add("vector", v_hmax)

    @dma_step
    def y_ar2a(sy):
        return sy.dma_start(out=c.cc2i[:, :],
                            in_=c.smalls[0:1, c.SM_HMAX:c.SM_HMAX + 1])
    sd.add("sync", y_ar2a)

    gpos2 = sd.mark("gpsimd")

    def g_ar2b(g):
        return g.collective_compute(
            "AllReduce", ALU.max, replica_groups=[list(range(NCORES))],
            ins=[c.cc2i], outs=[c.cc2o])
    sd.add("gpsimd", g_ar2b)
    RLX2 = {"gpsimd": gpos2}

    for bt in range(2):
        def t_rd(te, bt=bt):
            for q in range(4):
                ins = te.transpose(c.pb[6][:, q * 128:(q + 1) * 128],
                                   c.m1d[:, (bt * 4 + q) * 128:(bt * 4 + q + 1) * 128],
                                   c.ident[:, :])
            return ins
        sd.add("tensor", t_rd, relax=RLX2, extra=[("vwd1", NICC)])

        def v_rd(v, bt=bt):
            for q in range(4):
                ins = v.reduce_max(c.rsd[:, bt * 4 + q:bt * 4 + q + 1],
                                   c.pb[6][:, q * 128:(q + 1) * 128], axis=AX.X)
            return ins
        sd.add("vector", v_rd, relax=RLX2)

    sd.add("vector", lambda v: v.tensor_scalar_mul(
        c.invw_col[:, 0:8], c.rsd[:, 0:8], 1.0 / 127.0), relax=RLX2)
    sd.add("vector", lambda v: v.reciprocal(c.invw_col[:, 0:8],
                                            c.invw_col[:, 0:8]), relax=RLX2)

    for bt in range(2):
        def t_tinyd(te, bt=bt):
            for q in range(4):
                ins = te.transpose(c.pb[6][0:1, q * 128:(q + 1) * 128],
                                   c.invw_col[:, bt * 4 + q:bt * 4 + q + 1],
                                   c.ident[:, :])
            return ins
        sd.add("tensor", t_tinyd, relax=RLX2)

        def v_tinyd(v, bt=bt):
            return v.tensor_copy(c.winv_row[0:1, bt * 512:(bt + 1) * 512],
                                 c.pb[6][0:1, 0:512])
        sd.add("vector", v_tinyd, relax=RLX2)

    for q in range(2):
        def t_binvd(te, q=q):
            return te.matmul(c.pb[7][:, 0:512], c.ones1[:, :],
                             c.winv_row[0:1, q * 512:(q + 1) * 512],
                             start=True, stop=True)
        sd.add("tensor", t_binvd, relax=RLX2)

        def v_binvd(v, q=q):
            return v.tensor_copy(c.binvd[:, q * 512:(q + 1) * 512], c.pb[7][:, 0:512])
        sd.add("vector", v_binvd, relax=RLX2)

    wd2v = []
    for icc in range(NICC):
        def v_wd2(v, icc=icc):
            wb = c.wdbuf[icc % 3]
            v.tensor_mul(wb[:, :], wb[:, :], c.binvd[:, :])
            return v.tensor_scalar(c.qwd[:, icc * D:(icc + 1) * D], wb[:, :],
                                   RND, RND, ALU.add, ALU.subtract)
        wd2v.append(sd.add("vector", v_wd2, relax=RLX2,
                           extra=[(c.dwd2[icc % 3], 16 * (icc // 3 + 1))]))
    c.wd2_v_pos = wd2v

    sd.seg = "post2"

    @dma_step
    def y_ar2c(sy):
        return sy.dma_start(out=c.smalls[0:1, c.SM_HMAX:c.SM_HMAX + 1],
                            in_=c.cc2o[:, :])
    sd.add("sync", y_ar2c)

    sd.add("vector", lambda v: v.tensor_mul(c.scal1[0:1, 5:6], c.scal1[0:1, 3:4],
                                            c.smalls[0:1, c.SM_HMAX:c.SM_HMAX + 1]))
    sd.add("vector", lambda v: v.tensor_scalar_mul(c.scal1[0:1, 5:6],
                                                   c.scal1[0:1, 5:6], 1.0 / 127.0))
    sd.add("vector", lambda v: v.reciprocal(c.scal1[0:1, 6:7], c.scal1[0:1, 5:6]))

    def t_sdcol(te):
        te.matmul(c.pb[6][:, 0:1], c.ones1[:, :], c.scal1[0:1, 5:6],
                  start=True, stop=True)
        return te.matmul(c.pb[6][:, 4:5], c.ones1[:, :], c.scal1[0:1, 6:7],
                         start=True, stop=True)
    sd.add("tensor", t_sdcol)

    def v_sdcol(v):
        v.tensor_copy(c.smalls[:, c.SM_SCALED:c.SM_SCALED + 1], c.pb[6][:, 0:1])
        return v.tensor_copy(c.smalls[:, c.SM_INVSD:c.SM_INVSD + 1], c.pb[6][:, 4:5])
    sd.add("vector", v_sdcol)

    sd.add("vector", lambda v: v.tensor_scalar_mul(c.rsd[:, 0:8], c.rsd[:, 0:8],
                                                   1.0 / 127.0))
    sd.add("vector", lambda v: v.tensor_scalar(
        c.rsd[:, 0:8], c.rsd[:, 0:8],
        c.smalls[:, c.SM_SCALED:c.SM_SCALED + 1], None, ALU.mult))

    qhv = []
    for n in range(2 * NICC):
        icc, hf = divmod(n, 2)
        on_scalar = (n % 3 != 0)

        if on_scalar:
            def s_qh(sc, b=n % 4):
                hb = c.hre[b]
                sc.activation(hb[:, :], hb[:, :], AF.Identity,
                              bias=c.smalls[:, 12:13],
                              scale=c.smalls[:, c.SM_INVSD:c.SM_INVSD + 1])
                return sc.activation(hb[:, :], hb[:, :], AF.Identity,
                                     bias=c.smalls[:, 13:14])
            sd.add("scalar", s_qh, extra=[(c.dqh[n % 4], 16 * (n // 4 + 1))])

            def v_qh(v, icc=icc, hf=hf, b=n % 4):
                return v.tensor_scalar(
                    c.qh_ap(icc)[:, hf * 1024:(hf + 1) * 1024], c.hre[b][:, :],
                    127.0, -128.0, ALU.min, ALU.max)
            qhv.append(sd.add("vector", v_qh))
        else:
            def v_qh(v, icc=icc, hf=hf, b=n % 4):
                hb = c.hre[b]
                v.tensor_scalar(hb[:, :], hb[:, :],
                                c.smalls[:, c.SM_INVSD:c.SM_INVSD + 1],
                                RND, ALU.mult, ALU.add)
                v.tensor_scalar_sub(hb[:, :], hb[:, :], RND)
                return v.tensor_scalar(
                    c.qh_ap(icc)[:, hf * 1024:(hf + 1) * 1024], hb[:, :],
                    127.0, -128.0, ALU.min, ALU.max)
            qhv.append(sd.add("vector", v_qh,
                              extra=[(c.dqh[n % 4], 16 * (n // 4 + 1))]))
    c.mk["qh_done_v"] = sd.mark("vector")
    c.qh_v_pos = qhv
    c.mk["qwd_done_v"] = sd.mark("vector")
    c.mk["f_done_t"] = sd.mark("tensor")



def _emit_gpsimd(c, block):
    s = c.s

    @block.gpsimd
    def _(g):
        s["gi"].inc(g.memset(c.ident[:, :], 0.0), 1)
        g.wait_ge(s["gi"].h, 1)
        g.affine_select(out=c.ident[:, :], in_=c.ident[:, :],
                        compare_op=ALU.not_equal, fill=1.0, base=0,
                        pattern=[[-1, 128]], channel_multiplier=1)
        g.memset(c.ones1[:, :], 1.0)
        g.memset(c.ones1b[:, :], 1.0)
        g.memset(c.onesc[:, :], 1.0)
        g.memset(c.cexp_s[:, :], 1.0 / 32.0)
        g.memset(c.smalls[:, 0:12], 0.0)
        g.memset(c.smalls[:, 14:16], 0.0)
        g.memset(c.smalls[:, 12:13], RND)
        g.memset(c.smalls[:, 13:14], -RND)
        g.memset(c.scal1[:, :], 0.0)
        s["gi"].inc(g.memset(c.cexp_b[:, :], -3.0), 1)
        s["dg"].inc(g.dma_start(out=c.clip_sb[:, :], in_=c.clip[:, :]), 16)
        for wsb, wr in ((c.wq_sb, c.wq_r), (c.wk_sb, c.wk_r), (c.wv_sb, c.wv_r)):
            for dc in range(DC):
                s["dg"].inc(g.dma_start(out=wsb[:, dc * D:(dc + 1) * D],
                                        in_=wr[dc]), 16)
        for tt in range(NTT):
            if tt >= 3:
                g.wait_ge(s["sev"].h, _a_group_count(tt - 2))
                g.wait_ge(c.dxa[tt % 3].h, 128 * (tt // 3))
            for dc in range(DC):
                c.dxa[tt % 3].inc(g.dma_start(
                    out=c.bufx[tt % 3][:, dc * 512:(dc + 1) * 512],
                    in_=c.xT_r[dc, :, tt * 512:(tt + 1) * 512]), 16)
        if c.stage < 2:
            return
        g.wait_ge(s["tqkv"].h, NTT)
        g.wait_ge(s["sev"].h, A_GROUPS_TOTAL)
        for jt in range(NJT):
            for ic in range(NIC):
                pi = jt * NIC + ic
                if pi >= 4:
                    g.wait_ge(s["tAV"].h, pi - 3)
                    g.wait_ge(c.dvb[pi % 4].h, 16 * (pi // 4))
                c.dvb[pi % 4].inc(g.dma_start(
                    out=c.vbuf[pi % 4][:, :],
                    in_=c.v_sp[ic * 128:(ic + 1) * 128, :]), 16)
        if c.stage < 3:
            return
        c.sched.replay("gpsimd", g, seg="pre")
        c.sched.replay("gpsimd", g, seg="pre2")
        c.sched.replay("gpsimd", g, seg="post")
        g.wait_ge(c.dhsp[0].h, 16 * NICC * 2)
        g.wait_ge(c.dhsp[1].h, 16 * NICC * 2)
        for n in range(1, 2 * NICC, 2):
            icc, hf = divmod(n, 2)
            if n >= 4:
                g.wait_ge(c.chain["vector"].h, c.qh_v_pos[n - 4])
            c.dqh[n % 4].inc(g.dma_start(
                out=c.hre[n % 4][:, :],
                in_=c.h_sp[icc, :, hf * 1024:(hf + 1) * 1024]), 16)
        c.sched.replay("gpsimd", g, seg="post2")



def _emit_sync(c, block):
    s = c.s

    @block.sync
    def _(sy):
        for tc in range(32):
            sy.wait_ge(s["sev"].h, _vev_count(tc))
            if tc >= 2:
                sy.wait_ge(c.dvsp[tc % 2].h, 16 * (tc // 2))
            c.dvsp[tc % 2].inc(sy.dma_start(
                out=c.v_sp[tc * 128:(tc + 1) * 128, :],
                in_=c.vstage[tc % 2][:, :]), 16)
        if c.stage < 2:
            if c.taps:
                _emit_taps_a(c, sy)
            return
        if c.taps:
            sy.wait_ge(c.dvsp[0].h, 16 * 16)
            sy.wait_ge(c.dvsp[1].h, 16 * 16)
            s["dout2"].inc(sy.dma_start(out=c.dbg_vst[:, :],
                                        in_=c.vstage[1][:, :]), 16)
        sy.wait_ge(s["tqkv"].h, NTT)
        nqs = 0

        def _y_quarters(budget):
            nonlocal nqs
            while nqs < min(128, budget):
                if nqs >= 4:
                    sy.wait_ge(s["vwp"].h, nqs - 3)
                nq, hf = divmod(nqs, 2)
                wr = c.wgT_r if nq < 32 else c.wuT_r
                qq = nq % 32
                off = (qq % 4) * 704 + hf * 352
                c.dwp[nqs % 4].inc(sy.dma_start(
                    out=c.wp1[nqs % 4][:, :],
                    in_=wr[qq // 4, :, off:off + 352]), 16)
                nqs += 1

        for jt in range(NJT):
            _y_quarters(16 * jt + 4)
            if jt >= 1:
                sy.wait_ge(s["vOev"].h, jt)
            if jt >= 2:
                sy.wait_ge(c.dxr[jt % 2].h, 16 * DC * (jt // 2))
            for ec in range(DC):
                c.dxr[jt % 2].inc(sy.dma_start(
                    out=c.xresb[:, ec * 256:(ec + 1) * 256],
                    in_=c.xT_r[ec, :, jt * 256:(jt + 1) * 256]), 16)
            sy.wait_ge(s["vOev"].h, jt + 1)
            sy.wait_ge(s["dx2"].h, 16 * DC * jt)
            for ec in range(DC):
                s["dx2"].inc(sy.dma_start(
                    out=c.x2_sp[ec, :, jt * 256:(jt + 1) * 256],
                    in_=c.x2stage[:, ec * 256:(ec + 1) * 256]), 16)
        _y_quarters(128)
        if c.stage < 3:
            if c.taps:
                _emit_taps_b(c, sy)
            return
        c.sched.replay("sync", sy, seg="pre")
        sy.wait_ge(c.chain["tensor"].h, c.mk["rx_done_t"])
        for n in range(0, 64, 2):
            wi, q = divmod(n, 32)
            hc, hf = q // 4, q % 4
            wr = (c.wgT_r, c.wuT_r)[wi]
            if n >= 4:
                sy.wait_ge(c.chain["vector"].h, c.p2_v_pos[n - 4])
            c.dp2[n % 4].inc(sy.dma_start(
                out=c.wp2h[n % 4][:, :],
                in_=wr[hc, :, hf * 704:(hf + 1) * 704]), 16)
        c.sched.replay("sync", sy, seg="pre2")
        sy.wait_ge(c.chain["vector"].h, c.mk["qwu_done_v"])
        for it in range(NICC * 4):
            if it % 4 == 0:
                icc = it // 4
                if icc >= 2:
                    sy.wait_ge(s["vwd1"].h, icc - 1)
                    sy.wait_ge(c.dwd1[icc % 2].h, 16 * (icc // 2))
                c.dwd1[icc % 2].inc(sy.dma_start(out=c.wdbuf[icc % 2][:, :],
                                                 in_=c.wdT_r[icc]), 16)
            sy.wait_ge(s["vh"].h, it + 1)
            if it >= 2:
                sy.wait_ge(c.dhsp[it % 2].h, 16 * (it // 2))
            c.dhsp[it % 2].inc(sy.dma_start(
                out=c.h_sp[it // 4, :, (it % 4) * 512:(it % 4 + 1) * 512],
                in_=c.htile[it % 2][:, :]), 16)
        c.sched.replay("sync", sy, seg="post")
        sy.wait_ge(s["vwd1"].h, NICC)
        for icc in range(0, NICC, 2):
            if icc >= 3:
                sy.wait_ge(c.chain["vector"].h, c.wd2_v_pos[icc - 3])
            c.dwd2[icc % 3].inc(sy.dma_start(out=c.wdbuf[icc % 3][:, :],
                                             in_=c.wdT_r[icc]), 16)
        sy.wait_ge(c.dhsp[0].h, 16 * NICC * 2)
        sy.wait_ge(c.dhsp[1].h, 16 * NICC * 2)

        def _y_qh(n):
            icc, hf = divmod(n, 2)
            if n == 0:
                sy.wait_ge(c.chain["vector"].h, c.wd2_v_pos[-1])
            if n >= 4:
                sy.wait_ge(c.chain["vector"].h, c.qh_v_pos[n - 4])
            c.dqh[n % 4].inc(sy.dma_start(
                out=c.hre[n % 4][:, :],
                in_=c.h_sp[icc, :, hf * 1024:(hf + 1) * 1024]), 16)

        _y_qh(0)
        _y_qh(2)
        c.sched.replay("sync", sy, seg="post2")
        for n in range(4, 2 * NICC, 2):
            _y_qh(n)
        sy.wait_ge(c.chain["vector"].h, c.mk["qh_done_v"])
        for hc in range(DC):
            if hc >= 2:
                sy.wait_ge(s["vdn"].h, (hc - 1) * 4)
                sy.wait_ge(c.dx2b[hc % 2].h, 16 * (hc // 2))
            c.dx2b[hc % 2].inc(sy.dma_start(out=c.x2re2[hc % 2][:, :],
                                            in_=c.x2_sp[hc]), 16)
            sy.wait_ge(s["vdn"].h, (hc + 1) * 4)
            if hc >= 2:
                sy.wait_ge(c.dout[hc % 2].h, 16 * (hc // 2))
            c.dout[hc % 2].inc(sy.dma_start(
                out=c.outT[hc * 128:(hc + 1) * 128, :],
                in_=c.ostage[hc % 2][:, :]), 16)
        if c.taps:
            s["dout2"].inc(sy.dma_start(out=c.dbg_sc[:, 0:16], in_=c.smalls[:, :]), 16)
            s["dout2"].inc(sy.dma_start(out=c.dbg_sc[:, 16:38], in_=c.rsg[:, :]), 16)
            s["dout2"].inc(sy.dma_start(out=c.dbg_sc[:, 38:60], in_=c.rsu[:, :]), 16)
            s["dout2"].inc(sy.dma_start(out=c.dbg_sc[:, 60:68], in_=c.rsd[:, :]), 16)
            s["dout2"].inc(sy.dma_start(out=c.dbg_sc[0:1, 68:76], in_=c.scal1[0:1, :]), 16)
            sy.wait_ge(s["dout2"].h, s["dout2"].n)
        sy.wait_ge(c.dout[0].h, 16 * 4)
        sy.wait_ge(c.dout[1].h, 16 * 4)


def _emit_taps_a(c, sy):
    s = c.s
    sy.wait_ge(s["tqkv"].h, NTT)
    sy.wait_ge(s["sev"].h, A_GROUPS_TOTAL)
    n = 0
    for dst, src in ((c.dbg_q, c.q_sb), (c.dbg_k, c.k_sb), (c.dbg_mup, c.mup_row)):
        s["dout2"].inc(sy.dma_start(out=dst[:, :], in_=src[:, :]), 16)
        n += 16
    sy.wait_ge(s["dout2"].h, n)
    sy.wait_ge(c.dvsp[0].h, 16 * 16)
    sy.wait_ge(c.dvsp[1].h, 16 * 16)


def _emit_taps_b(c, sy):
    s = c.s
    sy.wait_ge(s["vst2"].h, NJT)
    sy.wait_ge(s["vmrow"].h, NJT)
    n = 0
    for i, row in enumerate((c.mu2_row, c.var2_row, c.m_row)):
        s["dout2"].inc(sy.dma_start(out=c.dbg_rows[:, i * TJ:(i + 1) * TJ],
                                    in_=row[:, :]), 16)
        n += 16
    sy.wait_ge(s["dout2"].h, n)
    sy.wait_ge(s["dx2"].h, 16 * DC * NJT)
    sy.wait_ge(c.dvsp[0].h, 16 * 16)
    sy.wait_ge(c.dvsp[1].h, 16 * 16)



def _emit_tensor(c, block):
    s = c.s

    @block.tensor
    def _(te):
        te.wait_ge(s["gi"].h, 2)
        te.wait_ge(s["dg"].h, 16 * 25)
        def _stats_x(tt):
            te.wait_ge(c.dxa[tt % 3].h, 128 * (tt // 3 + 1))
            if tt >= 1:
                te.wait_ge(s["vstat"].h, tt)
            ins = None
            for dc in range(DC):
                ins = te.matmul(c.pb[2][0:1, 0:512], c.onesc[:, :],
                                c.bufx[tt % 3][:, dc * 512:(dc + 1) * 512],
                                start=(dc == 0), stop=(dc == DC - 1))
            s["tsx"].inc(ins, 1)
            te.wait_ge(s["tsx"].h, s["tsx"].n)

        def _stats_sq(tt):
            te.wait_ge(s["ssq"].h, tt + 1)
            ins = None
            for dc in range(DC):
                xs = c.xsqA if dc < 4 else c.xsqB
                ins = te.matmul(c.pb[3][0:1, 0:512], c.onesc[:, :],
                                xs[:, (dc % 4) * 512:(dc % 4 + 1) * 512],
                                start=(dc == 0), stop=(dc == DC - 1))
            s["tstat"].inc(ins, 1)
            te.wait_ge(s["tstat"].h, s["tstat"].n)

        def _bcast(tt):
            te.wait_ge(s["sinv"].h, tt + 1)
            if tt >= 1:
                te.wait_ge(s["vbc"].h, tt)
            te.matmul(c.pb[7][:, 0:512], c.ones1[:, :],
                      c.scr_tmp[0:1, 0:512], start=True, stop=True)
            te.wait_ge(s["vmup"].h, tt + 1)
            ins = te.matmul(c.pb[6][:, 0:512], c.ones1b[:, :],
                            c.mup_row[0:1, tt * 512:(tt + 1) * 512],
                            start=True, stop=True)
            s["tbc"].inc(ins, 1)
            te.wait_ge(s["tbc"].h, s["tbc"].n)

        def _qkv(tt, mids):
            te.wait_ge(s["vxp"].h, tt + 1)
            gbase = _a_group_count(tt)
            for gi_, grp in enumerate(_a_groups(tt)):
                if gi_ in mids:
                    mids[gi_]()
                g = gbase + gi_
                bank = c.pb[(0, 1, 4, 5)[g % 4]]
                if g >= 4:
                    te.wait_ge(s["sev"].h, g - 3)
                kind = grp[0]
                if kind in ("q", "k"):
                    ec = grp[1]
                    wsb = c.wq_sb if kind == "q" else c.wk_sb
                    for dc in range(DC):
                        ins = te.matmul(bank[:, 0:512],
                                        wsb[:, dc * D + ec * 128:dc * D + (ec + 1) * 128],
                                        c.bufx[tt % 3][:, dc * 512:(dc + 1) * 512],
                                        start=(dc == 0), stop=(dc == DC - 1))
                else:
                    tc, eh = grp[1], grp[2]
                    for dc in range(DC):
                        ins = te.matmul(
                            bank[:, 0:512],
                            c.bufx[tt % 3][:, dc * 512 + (tc % 4) * 128:dc * 512 + (tc % 4 + 1) * 128],
                            c.wv_sb[:, dc * D + eh * 512:dc * D + (eh + 1) * 512],
                            start=(dc == 0), stop=(dc == DC - 1))
                s["tqg"].inc(ins, 1)
            s["tqkv"].inc(te.nop(), 1)

        for tt in range(NTT):
            _stats_x(tt)
            if tt == 0:
                _stats_sq(0)
                _bcast(0)
            else:
                nb = 10 if len(_a_groups(tt - 1)) == 24 else 8
                _qkv(tt - 1, {4: (lambda t=tt: _stats_sq(t)),
                              nb: (lambda t=tt: _bcast(t))})
        _qkv(NTT - 1, {})
        if c.stage < 2:
            return

        te.wait_ge(s["vstat"].h, NTT)
        te.wait_ge(s["sev"].h, A_GROUPS_TOTAL)
        for jt in range(NJT):
            epil = _b_epilogue_steps(c, jt - 1)
            epil_at = {8: 0, 12: 1, 20: 2, 26: 3}
            for ic in range(NIC):
                gS = jt * NIC + ic
                if gS >= 2:
                    te.wait_ge(s["sE"].h, gS - 1)
                ins = None
                for ec in range(DC):
                    ins = te.matmul(
                        c.pb[ic % 2][:, 0:256],
                        c.q_sb[:, ec * N + ic * 128:ec * N + (ic + 1) * 128],
                        c.k_sb[:, ec * TJ + jt * 256:ec * TJ + (jt + 1) * 256],
                        start=(ec == 0), stop=(ec == DC - 1))
                s["tS"].inc(ins, 1)
                if ic in epil_at and epil_at[ic] < len(epil):
                    epil[epil_at[ic]](te)
                if ic >= 2:
                    _b_av(c, te, jt, ic - 2)
            _b_av(c, te, jt, NIC - 2)
            _b_av(c, te, jt, NIC - 1)
            s["trr"].inc(te.nop(), 1)
            te.wait_ge(s["vrv"].h, jt + 1)
            if jt >= 1:
                te.wait_ge(s["vmrow"].h, jt)
            for q in range(2):
                ins = te.transpose(c.pb[7][0:1, q * 128:(q + 1) * 128],
                                   c.smalls[:, 14 + q:15 + q], c.ident[:, :])
            s["trt"].inc(ins, 1)
            te.wait_ge(s["trt"].h, s["trt"].n)
            te.wait_ge(s["vrt"].h, jt + 1)
            ins = te.matmul(c.pb[7][:, 0:256], c.ones1[:, :],
                            c.rinv_row[0:1, 0:256], start=True, stop=True)
            s["tb7"].inc(ins, 1)
        for step in _b_epilogue_steps(c, NJT - 1):
            step(te)
        if c.stage < 3:
            return

        c.sched.replay("tensor", te, seg="pre")
        c.sched.replay("tensor", te, seg="pre2")
        vch = c.chain["vector"]
        for icc in range(NICC):
            for t4 in range(4):
                it = icc * 4 + t4
                bg = c.pb[(it % 4) * 2]
                bu = c.pb[(it % 4) * 2 + 1]
                if it == 0:
                    te.wait_ge(vch.h, c.mk["qx_done_v"])
                if it >= 4:
                    te.wait_ge(s["sgu"].h, (it - 3) * 2)
                for hc in range(DC):
                    te.matmul(bg[:, 0:512],
                              c.qwg[:, hc * DI + icc * 128:hc * DI + (icc + 1) * 128],
                              c.qx[:, hc * TJ + t4 * 512:hc * TJ + (t4 + 1) * 512],
                              start=(hc == 0), stop=(hc == DC - 1))
                for hc in range(DC):
                    ins = te.matmul(bu[:, 0:512],
                                    c.qwu[:, hc * DI + icc * 128:hc * DI + (icc + 1) * 128],
                                    c.qx[:, hc * TJ + t4 * 512:hc * TJ + (t4 + 1) * 512],
                                    start=(hc == 0), stop=(hc == DC - 1))
                s["tgu"].inc(ins, 1)
        c.sched.replay("tensor", te, seg="post")
        c.sched.replay("tensor", te, seg="post2")
        for hc in range(DC):
            if hc == 0:
                te.wait_ge(vch.h, c.mk["qh_done_v"])
            if hc >= 2:
                te.wait_ge(s["sdn"].h, (hc - 1) * 4)
            ins = None
            for icc in range(NICC):
                for t4 in range(4):
                    ins = te.matmul(
                        c.pb[(hc % 2) * 4 + t4][:, 0:512],
                        c.qwd[:, icc * D + hc * 128:icc * D + (hc + 1) * 128],
                        c.qh_ap(icc)[:, t4 * 512:(t4 + 1) * 512],
                        start=(icc == 0), stop=(icc == NICC - 1))
            s["tdn"].inc(ins, 1)


def _b_av(c, te, jt, ic):
    s = c.s
    gAV = jt * NIC + ic
    te.wait_ge(s["sE"].h, gAV + 1)
    te.wait_ge(c.dvb[gAV % 4].h, 16 * (gAV // 4 + 1))
    if ic == 0:
        te.wait_ge(s["vOev"].h, jt)
        te.wait_ge(s["vrv"].h, jt)
    for ec in range(DC):
        te.matmul(c.pb[2 + ec // 2][:, (ec % 2) * 256:(ec % 2) * 256 + 256],
                  c.vbuf[gAV % 4][:, ec * 128:(ec + 1) * 128],
                  c.E[:, ic * 256:(ic + 1) * 256],
                  start=(ic == 0 and ec % 2 == 0),
                  stop=(ic == NIC - 1 and ec % 2 == 1),
                  skip_group_check=True)
    for jc in range(2):
        ins = te.matmul(c.pb[6][:, jc:jc + 1],
                        c.E[:, ic * 256 + jc * 128:ic * 256 + (jc + 1) * 128],
                        c.onesc[:, :],
                        start=(ic == 0 and jc == 0), stop=(ic == NIC - 1 and jc == 1),
                        skip_group_check=True)
    s["tAV"].inc(ins, 1)


def _b_epilogue_steps(c, jt):
    s = c.s
    if jt < 0:
        return []

    def st_stats_a(te):
        te.wait_ge(s["sx2a"].h, jt + 1)
        te.wait_ge(s["vb7"].h, jt + 1)
        ins = None
        for ec in range(DC):
            ins = te.matmul(c.pb[7][0:1, 256:512], c.onesc[:, :],
                            c.x2b[:, ec * 256:(ec + 1) * 256],
                            start=(ec == 0), stop=(ec == DC - 1))
        s["tst2a"].inc(ins, 1)
        te.wait_ge(s["tst2a"].h, s["tst2a"].n)

    def st_stats_b(te):
        te.wait_ge(s["sx2b"].h, jt + 1)
        te.wait_ge(s["vst2"].h, 2 * jt + 1)
        ins = None
        for ec in range(DC):
            ins = te.matmul(c.pb[7][0:1, 0:256], c.onesc[:, :],
                            c.x2b[:, ec * 256:(ec + 1) * 256],
                            start=(ec == 0), stop=(ec == DC - 1))
        s["tst2b"].inc(ins, 1)
        te.wait_ge(s["tst2b"].h, s["tst2b"].n)

    def st_mt_tp(te):
        te.wait_ge(s["vmt"].h, jt + 1)
        te.wait_ge(s["vst2"].h, 2 * jt + 2)
        for q in range(2):
            te.transpose(c.pb[7][:, q * 128:(q + 1) * 128],
                         c.Mt[:, q * 128:(q + 1) * 128], c.ident[:, :])
        ins = None
        for q in range(2):
            ins = te.transpose(c.pb[7][:, 256 + q * 128:256 + (q + 1) * 128],
                               c.mt[:, q * 128:(q + 1) * 128], c.ident[:, :])
        s["tmt"].inc(ins, 1)

    def st_mc_tp(te):
        te.wait_ge(s["vmc"].h, jt + 1)
        ins = None
        for q in range(2):
            te.transpose(c.pb[7][0:1, q * 128:(q + 1) * 128],
                         c.smalls[:, c.SM_MC + q:c.SM_MC + q + 1], c.ident[:, :])
        for q in range(2):
            ins = te.transpose(c.pb[7][0:1, 256 + q * 128:256 + (q + 1) * 128],
                               c.smalls[:, c.SM_MC + 2 + q:c.SM_MC + 3 + q],
                               c.ident[:, :])
        s["tmt"].inc(ins, 1)
        te.wait_ge(s["tmt"].h, s["tmt"].n)

    return [st_stats_a, st_stats_b, st_mt_tp, st_mc_tp]



def _emit_scalar(c, block):
    s = c.s

    @block.scalar
    def _(sc):
        def _sq(tt):
            sc.wait_ge(c.dxa[tt % 3].h, 128 * (tt // 3 + 1))
            if tt >= 1:
                sc.wait_ge(s["tstat"].h, tt)
            ins = None
            for dc in range(DC):
                xs = c.xsqA if dc < 4 else c.xsqB
                ins = sc.activation(xs[:, (dc % 4) * 512:(dc % 4 + 1) * 512],
                                    c.bufx[tt % 3][:, dc * 512:(dc + 1) * 512],
                                    AF.Square)
            s["ssq"].inc(ins, 1)

        def _sqrt(tt):
            sc.wait_ge(s["vstat"].h, tt + 1)
            ins = sc.activation(c.scr_tmp[0:1, 0:512], c.scr_var[0:1, 0:512], AF.Sqrt)
            s["sinv"].inc(ins, 1)

        def _evacs(tt, lo=0, hi=None):
            groups = _a_groups(tt)
            gbase = _a_group_count(tt)
            for gi_ in range(lo, len(groups) if hi is None else hi):
                grp = groups[gi_]
                g = gbase + gi_
                bank = c.pb[(0, 1, 4, 5)[g % 4]]
                sc.wait_ge(s["tqg"].h, g + 1)
                kind = grp[0]
                if kind == "q":
                    ec = grp[1]
                    dst = c.q_sb[:, ec * N + tt * 512:ec * N + (tt + 1) * 512]
                elif kind == "k":
                    ec = grp[1]
                    dst = c.k_sb[:, ec * TJ + tt * 512:ec * TJ + (tt + 1) * 512]
                else:
                    tc, eh = grp[1], grp[2]
                    if eh == 0 and tc >= 2:
                        sc.wait_ge(c.dvsp[tc % 2].h, 16 * (tc // 2))
                    dst = c.vstage[tc % 2][:, eh * 512:(eh + 1) * 512]
                ins = sc.activation(dst, bank[:, 0:512], AF.Identity)
                s["sev"].inc(ins, 1)

        _sq(0)
        _sq(1)
        _sqrt(0)
        for k in range(1, NTT):
            _evacs(k - 1, 0, 5)
            _sqrt(k)
            _evacs(k - 1, 5, 13)
            if k + 1 < NTT:
                _sq(k + 1)
            _evacs(k - 1, 13)
        _evacs(NTT - 1)
        if c.stage < 2:
            return

        for jt in range(NJT):
            for ic in range(NIC):
                if jt >= 1 and ic == 2:
                    sc.wait_ge(s["vOev"].h, jt)
                    ins = sc.activation(c.x2b[:, :], c.x2stage[:, :], AF.Identity)
                    s["sx2a"].inc(ins, 1)
                if jt >= 1 and ic == 10:
                    sc.wait_ge(s["tst2a"].h, jt)
                    ins = sc.activation(c.x2b[:, :], c.x2b[:, :], AF.Square)
                    s["sx2b"].inc(ins, 1)
                gS = jt * NIC + ic
                sc.wait_ge(s["tS"].h, gS + 1)
                ins = sc.activation(c.E[:, ic * 256:(ic + 1) * 256],
                                    c.pb[ic % 2][:, 0:256], AF.Exp,
                                    bias=c.cexp_b[:, :], scale=c.cexp_s[:, :])
                s["sE"].inc(ins, 1)
        sc.wait_ge(s["vOev"].h, NJT)
        ins = sc.activation(c.x2b[:, :], c.x2stage[:, :], AF.Identity)
        s["sx2a"].inc(ins, 1)
        sc.wait_ge(s["tst2a"].h, NJT)
        ins = sc.activation(c.x2b[:, :], c.x2b[:, :], AF.Square)
        s["sx2b"].inc(ins, 1)
        if c.stage < 3:
            return

        c.sched.replay("scalar", sc, seg="pre")
        sc.wait_ge(c.chain["tensor"].h, c.mk["rx_done_t"])
        for n in range(1, 64, 2):
            wi, q = divmod(n, 32)
            hc, hf = q // 4, q % 4
            wr = (c.wgT_r, c.wuT_r)[wi]
            if n >= 4:
                sc.wait_ge(c.chain["vector"].h, c.p2_v_pos[n - 4])
            c.dp2[n % 4].inc(sc.dma_start(
                out=c.wp2h[n % 4][:, :],
                in_=wr[hc, :, hf * 704:(hf + 1) * 704]), 16)
        c.sched.replay("scalar", sc, seg="pre2")
        for icc in range(NICC):
            for t4 in range(4):
                it = icc * 4 + t4
                bg = c.pb[(it % 4) * 2]
                bu = c.pb[(it % 4) * 2 + 1]
                sc.wait_ge(s["tgu"].h, it + 1)
                if it >= 1:
                    sc.wait_ge(s["vh"].h, it)
                sc.activation(c.sg[:, :], bg[:, 0:512], AF.Silu,
                              scale=c.rsg[:, icc:icc + 1])
                ins = sc.activation(c.su[:, :], bu[:, 0:512], AF.Identity,
                                    scale=c.rsu[:, icc:icc + 1])
                s["sgu"].inc(ins, 2)
        c.sched.replay("scalar", sc, seg="post")
        sc.wait_ge(s["vwd1"].h, NICC)
        for icc in range(1, NICC, 2):
            if icc >= 3:
                sc.wait_ge(c.chain["vector"].h, c.wd2_v_pos[icc - 3])
            c.dwd2[icc % 3].inc(sc.dma_start(out=c.wdbuf[icc % 3][:, :],
                                             in_=c.wdT_r[icc]), 16)
        c.sched.replay("scalar", sc, seg="post2")
        for hc in range(DC):
            sc.wait_ge(s["tdn"].h, hc + 1)
            if hc >= 2:
                sc.wait_ge(c.dout[hc % 2].h, 16 * (hc // 2))
            for t4 in range(4):
                ins = sc.activation(
                    c.ostage[hc % 2][:, t4 * 512:(t4 + 1) * 512],
                    c.pb[(hc % 2) * 4 + t4][:, 0:512], AF.Identity,
                    scale=c.rsd[:, hc:hc + 1])
                s["sdn"].inc(ins, 1)



def _emit_vector(c, block):
    s = c.s

    @block.vector
    def _(v):
        for tt in range(NTT):
            v.wait_ge(s["tstat"].h, tt + 1)
            if tt >= 1:
                v.wait_ge(s["tbc"].h, tt)
                v.wait_ge(s["vmup"].h, tt)
            sb0, sb1 = c.pb[2], c.pb[3]
            v.tensor_scalar_mul(c.scr_mu[0:1, 0:512], sb0[0:1, 0:512], 1.0 / D)
            v.tensor_scalar_mul(c.scr_var[0:1, 0:512], sb1[0:1, 0:512], 1.0 / D)
            v.tensor_mul(c.scr_tmp[0:1, 0:512], c.scr_mu[0:1, 0:512],
                         c.scr_mu[0:1, 0:512])
            v.tensor_sub(c.scr_var[0:1, 0:512], c.scr_var[0:1, 0:512],
                         c.scr_tmp[0:1, 0:512])
            v.tensor_scalar_add(c.scr_var[0:1, 0:512], c.scr_var[0:1, 0:512], EPS)
            ins = v.reciprocal(c.scr_var[0:1, 0:512], c.scr_var[0:1, 0:512])
            s["vstat"].inc(ins, 1)
            v.wait_ge(s["sinv"].h, tt + 1)
            ins = v.tensor_mul(c.mup_row[0:1, tt * 512:(tt + 1) * 512],
                               c.scr_mu[0:1, 0:512], c.scr_tmp[0:1, 0:512])
            s["vmup"].inc(ins, 1)
            v.wait_ge(s["tbc"].h, tt + 1)
            v.tensor_copy(c.invsig_bc[:, :], c.pb[7][:, 0:512])
            ins = v.tensor_copy(c.mup_bc[:, :], c.pb[6][:, 0:512])
            s["vbc"].inc(ins, 1)
            ins = None
            for dc in range(DC):
                sl = slice(dc * 512, (dc + 1) * 512)
                v.tensor_mul(c.bufx[tt % 3][:, sl], c.bufx[tt % 3][:, sl],
                             c.invsig_bc[:, :])
                ins = v.tensor_sub(c.bufx[tt % 3][:, sl], c.bufx[tt % 3][:, sl],
                                   c.mup_bc[:, :])
            s["vxp"].inc(ins, 1)
        if c.stage < 2:
            return

        v.wait_ge(s["tqkv"].h, NTT)
        nq = 0

        def _pass1(k):
            nonlocal nq
            for _ in range(k):
                if nq < 128:
                    v.wait_ge(c.dwp[nq % 4].h, 16 * (nq // 4 + 1))
                    q, hf = divmod(nq, 2)
                    m1 = c.m1g if q < 32 else c.m1u
                    qq = q % 32
                    off = (qq % 4) * 704 + hf * 352
                    dst = m1[:, off:off + 352]
                    wp = c.wp1[nq % 4]
                    if qq // 4 == 0:
                        v.tensor_copy(dst, wp[:, :])
                        v.tensor_scalar_mul(wp[:, :], wp[:, :], -1.0)
                        ins = v.tensor_max(dst, dst, wp[:, :])
                    else:
                        v.tensor_max(dst, dst, wp[:, :])
                        v.tensor_scalar_mul(wp[:, :], wp[:, :], -1.0)
                        ins = v.tensor_max(dst, dst, wp[:, :])
                    s["vwp"].inc(ins, 1)
                    nq += 1

        def _epilogue_v(jt):
            sl2 = slice(jt * 256, (jt + 1) * 256)
            v.wait_ge(s["tst2a"].h, jt + 1)
            ins = v.tensor_scalar_mul(c.mu2_row[0:1, sl2],
                                      c.pb[7][0:1, 256:512], 1.0 / D)
            s["vst2"].inc(ins, 1)
            v.wait_ge(s["tst2b"].h, jt + 1)
            v.tensor_scalar_mul(c.var2_row[0:1, sl2], c.pb[7][0:1, 0:256], 1.0 / D)
            v.tensor_mul(c.trow_s[0:1, :], c.mu2_row[0:1, sl2], c.mu2_row[0:1, sl2])
            ins = v.tensor_sub(c.var2_row[0:1, sl2], c.var2_row[0:1, sl2],
                               c.trow_s[0:1, :])
            s["vst2"].inc(ins, 1)
            v.wait_ge(s["tmt"].h, 2 * jt + 1)
            for q in range(2):
                v.reduce_max(c.smalls[:, c.SM_MC + q:c.SM_MC + q + 1],
                             c.pb[7][:, q * 128:(q + 1) * 128], axis=AX.X)
            ins = None
            for q in range(2):
                ins = v.tensor_reduce(c.smalls[:, c.SM_MC + 2 + q:c.SM_MC + 3 + q],
                                      c.pb[7][:, 256 + q * 128:256 + (q + 1) * 128],
                                      axis=AX.X, op=ALU.min)
            s["vmc"].inc(ins, 1)
            v.wait_ge(s["tmt"].h, 2 * jt + 2)
            v.tensor_copy(c.Mrow_s[0:1, :], c.pb[7][0:1, 0:256])
            v.tensor_copy(c.mrow_s[0:1, :], c.pb[7][0:1, 256:512])
            v.tensor_sub(c.Mrow_s[0:1, :], c.Mrow_s[0:1, :], c.mu2_row[0:1, sl2])
            v.tensor_sub(c.mrow_s[0:1, :], c.mu2_row[0:1, sl2], c.mrow_s[0:1, :])
            ins = v.tensor_max(c.m_row[0:1, sl2], c.Mrow_s[0:1, :], c.mrow_s[0:1, :])
            s["vmrow"].inc(ins, 1)

        for jt in range(NJT):
            if jt >= 1:
                _epilogue_v(jt - 1)
            v.wait_ge(s["trr"].h, jt + 1)
            ins = v.reciprocal(c.smalls[:, 14:16], c.pb[6][:, 0:2])
            s["vrv"].inc(ins, 1)
            v.wait_ge(s["trt"].h, jt + 1)
            ins = v.tensor_copy(c.rinv_row[0:1, 0:256], c.pb[7][0:1, 0:256])
            s["vrt"].inc(ins, 1)
            v.wait_ge(s["tb7"].h, jt + 1)
            ins = v.tensor_copy(c.rinv_bc[:, :], c.pb[7][:, 0:256])
            s["vb7"].inc(ins, 1)
            v.wait_ge(s["tAV"].h, (jt + 1) * NIC)
            if jt >= 1:
                v.wait_ge(s["dx2"].h, 16 * DC * jt)
                v.wait_ge(s["sx2b"].h, jt)
            v.wait_ge(c.dxr[jt % 2].h, 16 * DC * (jt // 2 + 1))
            for ec in range(DC):
                sl = slice(ec * 256, (ec + 1) * 256)
                v.tensor_mul(c.x2stage[:, sl],
                             c.pb[2 + ec // 2][:, (ec % 2) * 256:(ec % 2) * 256 + 256],
                             c.rinv_bc[:, :])
                ins = v.tensor_add(c.x2stage[:, sl], c.x2stage[:, sl],
                                   c.xresb[:, sl])
                if ec == DC - 1:
                    s["vOev"].inc(ins, 1)
            v.memset(c.Mt[:, :], -1e30)
            v.memset(c.mt[:, :], 1e30)
            ins = None
            for ec in range(DC):
                sl = slice(ec * 256, (ec + 1) * 256)
                v.tensor_max(c.Mt[:, :], c.Mt[:, :], c.x2stage[:, sl])
                ins = v.tensor_tensor(c.mt[:, :], c.mt[:, :], c.x2stage[:, sl],
                                      ALU.min)
            s["vmt"].inc(ins, 1)
            _pass1(16)
        _epilogue_v(NJT - 1)
        if c.stage < 3:
            return

        c.sched.replay("vector", v, seg="pre")
        c.sched.replay("vector", v, seg="pre2")

        def _wd1(icc):
            v.wait_ge(c.dwd1[icc % 2].h, 16 * (icc // 2 + 1))
            wb = c.wdbuf[icc % 2]
            if icc == 0:
                v.tensor_copy(c.m1d[:, :], wb[:, :])
                v.tensor_scalar_mul(wb[:, :], wb[:, :], -1.0)
                ins = v.tensor_max(c.m1d[:, :], c.m1d[:, :], wb[:, :])
            else:
                v.tensor_max(c.m1d[:, :], c.m1d[:, :], wb[:, :])
                v.tensor_scalar_mul(wb[:, :], wb[:, :], -1.0)
                ins = v.tensor_max(c.m1d[:, :], c.m1d[:, :], wb[:, :])
            s["vwd1"].inc(ins, 1)

        for icc in range(NICC):
            for t4 in range(4):
                it = icc * 4 + t4
                v.wait_ge(s["sgu"].h, (it + 1) * 2)
                if it >= 2:
                    v.wait_ge(c.dhsp[it % 2].h, 16 * (it // 2))
                ht = c.htile[it % 2]
                ins = v.tensor_mul(ht[:, :], c.sg[:, :], c.su[:, :])
                s["vh"].inc(ins, 1)
                ins = v.reduce_max(c.hmcols[:, it:it + 1], ht[:, :], axis=AX.X,
                                   apply_absolute_value=True)
                if it == NICC * 4 - 1:
                    s["vh"].inc(ins, 1)
                if it % 4 == 2:
                    _wd1(it // 4)
        v.wait_ge(s["vh"].h, NICC * 4 + 1)
        ins = v.reduce_max(c.hmax_col[:, :], c.hmcols[:, 0:NICC * 4], axis=AX.X)
        s["vh"].inc(ins, 1)
        c.sched.replay("vector", v, seg="post")
        c.sched.replay("vector", v, seg="post2")
        v.wait_ge(c.chain["vector"].h, 0)
        for hc in range(DC):
            v.wait_ge(c.dx2b[hc % 2].h, 16 * (hc // 2 + 1))
            for t4 in range(4):
                v.wait_ge(s["sdn"].h, hc * 4 + t4 + 1)
                sl = slice(t4 * 512, (t4 + 1) * 512)
                ins = v.tensor_add(c.ostage[hc % 2][:, sl],
                                   c.ostage[hc % 2][:, sl],
                                   c.x2re2[hc % 2][:, sl])
                s["vdn"].inc(ins, 1)



_PROG_CACHE = {}


def _get_program():
    if "full" not in _PROG_CACHE:
        _PROG_CACHE["full"] = build_program(stage=3, taps=False)
    return _PROG_CACHE["full"]


def make_in_maps(x, wq, wk, wv, wgT, wuT, wdT, clip):
    in_maps = []
    for cid in range(NCORES):
        b, jh = cid >> 1, cid & 1
        xb = x[b]
        xr = xb if jh == 0 else np.concatenate([xb[TJ:], xb[:TJ]], axis=0)
        in_maps.append({
            "xT": np.ascontiguousarray(xr.T),
            "wq": wq, "wk": wk, "wv": wv,
            "wgT": wgT, "wuT": wuT, "wdT": wdT,
            "clip": clip,
        })
    return in_maps


def kernel(L0_attn_x, L0_attn_W_query, L0_attn_W_key, L0_attn_W_value,
           L0_ffn_clip_x, L0_ffn_clip_down, L0_ffn_w_gate, L0_ffn_w_up,
           L0_ffn_w_down):
    x = np.ascontiguousarray(np.asarray(L0_attn_x, dtype=np.float32))
    wq = np.ascontiguousarray(np.asarray(L0_attn_W_query, dtype=np.float32))
    wk = np.ascontiguousarray(np.asarray(L0_attn_W_key, dtype=np.float32))
    wv = np.ascontiguousarray(np.asarray(L0_attn_W_value, dtype=np.float32))
    wgT = np.ascontiguousarray(np.asarray(L0_ffn_w_gate, dtype=np.float32).T)
    wuT = np.ascontiguousarray(np.asarray(L0_ffn_w_up, dtype=np.float32).T)
    wdT = np.ascontiguousarray(np.asarray(L0_ffn_w_down, dtype=np.float32).T)
    clip = np.array([[np.asarray(L0_ffn_clip_x).reshape(-1)[0],
                      np.asarray(L0_ffn_clip_down).reshape(-1)[0]]],
                    dtype=np.float32)
    nc = _get_program()
    res = run_bass_kernel_spmd(nc, make_in_maps(x, wq, wk, wv, wgT, wuT, wdT, clip),
                               core_ids=list(range(NCORES)))
    out = np.empty((4, N, D), dtype=np.float32)
    for cid, r in enumerate(res.results):
        b, jh = cid >> 1, cid & 1
        out[b, jh * TJ:(jh + 1) * TJ] = r["outT"].T
    return out


# version 24
# speedup vs baseline: 1.0282x; 1.0282x over previous
import contextlib
import numpy as np

import concourse.bass as bass
import concourse.mybir as mybir
from concourse.bass_utils import run_bass_kernel_spmd

F32 = mybir.dt.float32
BF16 = mybir.dt.bfloat16
AF = mybir.ActivationFunctionType
ALU = mybir.AluOpType
AX = mybir.AxisListType

RND = 12582912.0
EPS = 1e-5
D = 1024
N = 4096
TJ = 2048
DI = 2816
NCORES = 8
DC = 8
NTT = 8
NJT = 8
NIC = 32
NICC = 22
KBB = 1024


class _C:
    pass


class Sem:
    def __init__(self, es, nc, name):
        self.h = es.enter_context(nc.semaphore(name))
        self.n = 0

    def inc(self, instr, k=1):
        instr.then_inc(self.h, k)
        self.n += k
        return instr


ENGS = ("tensor", "scalar", "vector", "gpsimd", "sync")


class Sched:

    def __init__(self, c):
        self.c = c
        self.items = {e: [] for e in ENGS}
        self.seg = "pre"
        self.replayed = {e: 0 for e in ENGS}
        self.emitted_pos = {e: 0 for e in ENGS}

    def add(self, eng, emit, relax=None, extra=None):
        c = self.c
        waits = {}
        for e in ENGS:
            if e == eng:
                continue
            tgt = c.chain[e].n if relax is None or e not in relax else relax[e]
            if tgt > c.chain_seen[eng][e]:
                waits[e] = tgt
        self.items[eng].append((dict(waits), emit, extra or [], self.seg))
        for e, v in waits.items():
            c.chain_seen[eng][e] = v
        c.chain[eng].n += 16 if getattr(emit, "is_dma", False) else 1
        return c.chain[eng].n

    def mark(self, eng):
        return self.c.chain[eng].n

    def replay(self, eng, e, seg=None):
        c = self.c
        start = self.replayed[eng]
        idx = start
        own = self.emitted_pos[eng]
        for waits, emit, extra, iseg in self.items[eng][start:]:
            if seg is not None and iseg != seg:
                break
            e.wait_ge(c.chain[eng].h, c.chain[eng].n if False else own)
            for oe, v in waits.items():
                e.wait_ge(c.chain[oe].h, v)
            for sname, v in extra:
                h = sname.h if isinstance(sname, Sem) else c.s[sname].h
                e.wait_ge(h, v)
            ins = emit(e)
            k = 16 if getattr(emit, "is_dma", False) else 1
            ins.then_inc(c.chain[eng].h, k)
            own += k
            idx += 1
        self.replayed[eng] = idx
        self.emitted_pos[eng] = own


def dma_step(fn):
    fn.is_dma = True
    return fn



def build_program(stage=3, taps=False):
    nc = bass.Bass("TRN2", target_bir_lowering=False, debug=False,
                   detect_race_conditions=False)
    c = _C()
    c.nc = nc
    c.stage = stage
    c.taps = taps
    es = contextlib.ExitStack()

    c.xT = nc.dram_tensor("xT", [D, N], F32, kind="ExternalInput").ap()
    c.wq_d = nc.dram_tensor("wq", [D, D], F32, kind="ExternalInput").ap()
    c.wk_d = nc.dram_tensor("wk", [D, D], F32, kind="ExternalInput").ap()
    c.wv_d = nc.dram_tensor("wv", [D, D], F32, kind="ExternalInput").ap()
    c.wgT = nc.dram_tensor("wgT", [D, DI], F32, kind="ExternalInput").ap()
    c.wuT = nc.dram_tensor("wuT", [D, DI], F32, kind="ExternalInput").ap()
    c.wdT = nc.dram_tensor("wdT", [DI, D], F32, kind="ExternalInput").ap()
    c.clip = nc.dram_tensor("clip", [1, 2], F32, kind="ExternalInput").ap()

    sk = "ExternalOutput" if taps else "Internal"
    c.v_sp = nc.dram_tensor("v_sp", [N, D], BF16, kind=sk).ap()
    c.x2_sp = nc.dram_tensor("x2_sp", [DC, 128, TJ], F32, kind=sk).ap()
    c.h_sp = nc.dram_tensor("h_sp", [NICC, 128, TJ], F32,
                            kind=sk if stage >= 3 else "Internal").ap()
    c.cc1i = nc.dram_tensor("cc1i", [1, 1], F32, kind="Internal").ap()
    c.cc1o = nc.dram_tensor("cc1o", [1, 1], F32, kind="Internal").ap()
    c.cc2i = nc.dram_tensor("cc2i", [1, 1], F32, kind="Internal").ap()
    c.cc2o = nc.dram_tensor("cc2o", [1, 1], F32, kind="Internal").ap()
    if stage >= 3:
        c.outT = nc.dram_tensor("outT", [D, TJ], F32, kind="ExternalOutput").ap()
    if taps:
        c.dbg_q = nc.dram_tensor("dbg_q", [128, DC * N], BF16, kind="ExternalOutput").ap()
        c.dbg_k = nc.dram_tensor("dbg_k", [128, DC * TJ], BF16, kind="ExternalOutput").ap()
        c.dbg_mup = nc.dram_tensor("dbg_mup", [1, N], BF16, kind="ExternalOutput").ap()
        c.dbg_rows = nc.dram_tensor("dbg_rows", [1, 4 * TJ], F32, kind="ExternalOutput").ap()
        c.dbg_vst = nc.dram_tensor("dbg_vst", [128, D], BF16, kind="ExternalOutput").ap()
        if stage >= 3:
            c.dbg_qx = nc.dram_tensor("dbg_qx", [128, DC * TJ], BF16, kind="ExternalOutput").ap()
            c.dbg_sc = nc.dram_tensor("dbg_sc", [128, 80], F32, kind="ExternalOutput").ap()

    c.xT_r = c.xT.rearrange("(c p) t -> c p t", p=128)
    c.wq_r = c.wq_d.rearrange("(c p) e -> c p e", p=128)
    c.wk_r = c.wk_d.rearrange("(c p) e -> c p e", p=128)
    c.wv_r = c.wv_d.rearrange("(c p) e -> c p e", p=128)
    c.wgT_r = c.wgT.rearrange("(c p) i -> c p i", p=128)
    c.wuT_r = c.wuT.rearrange("(c p) i -> c p i", p=128)
    c.wdT_r = c.wdT.rearrange("(c p) h -> c p h", p=128)

    base = (nc.sbuf_base + 63) & ~63

    def sb(name, shape, dtype, off_kb):
        return nc.alloc_sbuf_tensor_at(
            name, shape, dtype, offset=base + int(off_kb * KBB)).ap()

    c.ident = sb("ident", [128, 128], F32, 0.0)
    c.ones1 = sb("ones1", [1, 128], F32, 0.5)
    c.onesc = sb("onesc", [128, 1], BF16, 1.0)
    c.cexp_s = sb("cexp_s", [128, 1], F32, 1.0625)
    c.cexp_b = sb("cexp_b", [128, 1], F32, 1.125)
    c.clip_sb = sb("clip_sb", [1, 2], F32, 1.1875)
    c.smalls = sb("smalls", [128, 16], F32, 1.25)
    c.SM_SCALEX, c.SM_SCALED, c.SM_INVSD, c.SM_HMAX, c.SM_LNR = 0, 1, 2, 3, 4
    c.SM_MC = 8
    c.rsg = sb("rsg", [128, 22], F32, 1.3125)
    c.rsu = sb("rsu", [128, 22], F32, 1.40625)
    c.rsd = sb("rsd", [128, 8], F32, 1.5)
    c.invw_col = sb("invw_col", [128, 22], F32, 1.53125)
    c.scal1 = sb("scal1", [1, 8], F32, 1.625)
    c.hmax_col = sb("hmax_col", [128, 1], F32, 1.6875)
    c.ones1b = sb("ones1b", [1, 128], BF16, 1.75)
    c.r_row = sb("r_row", [1, 256], F32, 2.0)
    c.rinv_row = sb("rinv_row", [1, 256], F32, 3.0)
    c.mup_row = sb("mup_row", [1, N], BF16, 4.0)
    c.var2_row = sb("var2_row", [1, TJ], F32, 4.0)
    c.b_row = c.var2_row
    c.scr_mu = sb("scr_mu", [1, 512], F32, 12.0)
    c.scr_var = sb("scr_var", [1, 512], F32, 14.0)
    c.scr_tmp = sb("scr_tmp", [1, 512], F32, 16.0)
    c.mu2_row = sb("mu2_row", [1, TJ], F32, 12.0)
    c.mup_bc = sb("mup_bc", [128, 512], F32, 18.0)
    c.m_row = sb("m_row", [1, TJ], F32, 20.0)
    c.a_row = c.m_row
    c.winv_row = sb("winv_row", [1, DI], F32, 28.0)
    c.invsig_bc = sb("invsig_bc", [128, 512], F32, 24.0)
    c.vstage = [sb(f"vstage{i}", [128, D], BF16, 39.0 + 2 * i) for i in range(2)]
    c.Mrow_s = sb("Mrow_s", [1, 256], F32, 39.0)
    c.mrow_s = sb("mrow_s", [1, 256], F32, 40.0)
    c.trow_s = sb("trow_s", [1, 256], F32, 41.0)
    c.hmcols = sb("hmcols", [128, NICC * 4 + 8], F32, 42.0)
    c.q_sb = sb("q_sb", [128, DC * N], BF16, 43.0)
    c.qwg = sb("qwg", [128, DC * DI], BF16, 43.0)
    c.qh1 = sb("qh1", [128, 11 * TJ], BF16, 43.0)
    c.qwu = sb("qwu", [128, DC * DI], BF16, 87.0)
    c.qwd = sb("qwd", [128, NICC * D], BF16, 87.0)
    c.k_sb = sb("k_sb", [128, DC * TJ], BF16, 107.0)
    c.qx = sb("qx", [128, DC * TJ], BF16, 131.0)
    c.qh2 = sb("qh2", [128, 11 * TJ], BF16, 131.0)
    c.wq_sb = sb("wq_sb", [128, DC * D], BF16, 139.0)
    c.wk_sb = sb("wk_sb", [128, DC * D], BF16, 155.0)
    c.wv_sb = sb("wv_sb", [128, DC * D], BF16, 171.0)
    c.bufx = [sb(f"bufx{i}", [128, DC * 512], BF16, 187.0 + 8 * i) for i in range(2)]
    c.bufx.append(sb("bufx2", [128, DC * 512], BF16, 30.0))
    c.xsqA = sb("xsqA", [128, 4 * 512], BF16, 203.0)
    c.xsqB = sb("xsqB", [128, 4 * 512], BF16, 26.0)
    c.E = sb("E", [128, NIC * 256], BF16, 139.0)
    c.x2stage = sb("x2stage", [128, TJ], F32, 155.0)
    c.x2b = sb("x2b", [128, TJ], BF16, 163.0)
    c.vbuf = [sb(f"vbuf{i}", [128, D], BF16, 167.0 + 2 * i) for i in range(4)]
    c.xresb = sb("xresb", [128, TJ], F32, 30.0)
    c.rinv_bc = sb("rinv_bc", [128, 256], F32, 177.0)
    c.Mt = sb("Mt", [128, 256], F32, 178.0)
    c.mt = sb("mt", [128, 256], F32, 179.0)
    c.m1g = sb("m1g", [128, DI], F32, 180.0)
    c.m1u = sb("m1u", [128, DI], F32, 191.0)
    c.wp1 = [sb(f"wp1{i}", [128, 352], F32, 202.0 + 1.375 * i)
             for i in range(4)]
    c.binv = sb("binv", [128, DI], F32, 163.0)
    c.a_bc = sb("a_bc", [128, 1024], F32, 175.0)
    c.b_bc = sb("b_bc", [128, 1024], F32, 179.0)
    c.x2re = [sb(f"x2re{i}", [128, 1024], F32, 183.0 + 4 * i) for i in range(2)]
    c.wp2h = [sb(f"wp2{i}", [128, DI // 4], F32, 191.0 + 2.75 * i)
              for i in range(4)]
    c.htile = [sb(f"htile{i}", [128, 512], F32, 183.0 + 2 * i) for i in range(2)]
    c.sg = sb("sg", [128, 512], F32, 187.0)
    c.su = sb("su", [128, 512], F32, 189.0)
    c.wdbuf = [sb(f"wdbuf{i}", [128, D], F32, 163.0 + 4 * i) for i in range(3)]
    c.m1d = sb("m1d", [128, D], F32, 171.0)
    c.binvd = sb("binvd", [128, D], F32, 175.0)
    c.hre = [sb(f"hre{i}", [128, TJ // 2], F32, 175.0 + 4 * i)
             for i in range(4)]
    c.x2re2 = [sb("x2re2a", [128, TJ], F32, 175.0),
               sb("x2re2b", [128, TJ], F32, 183.0)]
    c.ostage = [sb(f"ostage{i}", [128, TJ], F32, 191.0 + 8 * i) for i in range(2)]

    def qh_ap(icc):
        return (c.qh1[:, icc * TJ:(icc + 1) * TJ] if icc < 11
                else c.qh2[:, (icc - 11) * TJ:(icc - 10) * TJ])
    c.qh_ap = qh_ap

    c.pb = [nc.alloc_psum_tensor(f"pb{i}", [128, 512], F32) for i in range(8)]

    names = [
        "gi", "dg", "ssq", "tstat", "tsx", "vstat", "sinv", "vmup", "tbc",
        "vbc", "vxp", "tqg", "sev", "tqkv",
        "tS", "sE", "tAV", "vOev", "trr", "vrv", "trt", "vrt", "tb7", "vb7",
        "vxr", "dx2", "sx2a", "tst2a", "sx2b", "tst2b", "vst2",
        "tmt", "vmt", "vmc", "vmrow", "vwp",
        "tgu", "sgu", "vh", "vwd1",
        "tdn", "sdn", "vdn", "dout2",
    ]
    c.s = {n: Sem(es, nc, n) for n in names}
    c.dxa = [Sem(es, nc, f"dxa{i}") for i in range(3)]
    c.dvb = [Sem(es, nc, f"dvb{i}") for i in range(4)]
    c.dvsp = [Sem(es, nc, f"dvsp{i}") for i in range(2)]
    c.dxr = [Sem(es, nc, f"dxr{i}") for i in range(2)]
    c.dhsp = [Sem(es, nc, f"dhsp{i}") for i in range(2)]
    c.dwp = [Sem(es, nc, f"dwp{i}") for i in range(4)]
    c.dp2 = [Sem(es, nc, f"dp2{i}") for i in range(4)]
    c.dwd2 = [Sem(es, nc, f"dwd2{i}") for i in range(3)]
    c.dqh = [Sem(es, nc, f"dqh{i}") for i in range(4)]
    c.dwd1 = [Sem(es, nc, f"dwd1{i}") for i in range(2)]
    c.dx2b = [Sem(es, nc, f"dx2b{i}") for i in range(2)]
    c.dout = [Sem(es, nc, f"dout{i}") for i in range(2)]
    c.chain = {e: Sem(es, nc, f"ch_{e}") for e in ENGS}
    c.chain_seen = {e: {o: 0 for o in ENGS} for e in ENGS}
    c.sched = Sched(c)

    c.mk = {}

    _schedule_f(c)

    with es:
        with nc.Block() as block:
            _emit_gpsimd(c, block)
            _emit_sync(c, block)
            _emit_tensor(c, block)
            _emit_scalar(c, block)
            _emit_vector(c, block)
    return nc



def _a_groups(tt):
    gs = []
    for ec in range(8):
        gs.append(("q", ec, tt))
    if tt < 4:
        for ec in range(8):
            gs.append(("k", ec, tt))
    for tc in range(tt * 4, tt * 4 + 4):
        for eh in range(2):
            gs.append(("v", tc, eh))
    return gs


def _a_group_count(tt):
    return sum(len(_a_groups(t)) for t in range(tt))


A_GROUPS_TOTAL = sum(len(_a_groups(t)) for t in range(NTT))


def _vev_count(tc):
    tt = tc // 4
    for gi_, g in enumerate(_a_groups(tt)):
        if g[0] == "v" and g[1] == tc and g[2] == 1:
            return _a_group_count(tt) + gi_ + 1
    raise AssertionError



def _schedule_f(c):
    if c.stage < 3:
        return
    sd = c.sched
    s = c.s

    def v_rows1(v):
        v.tensor_scalar_add(c.var2_row[0:1, :], c.var2_row[0:1, :], EPS)
        v.reciprocal(c.var2_row[0:1, :], c.var2_row[0:1, :])
        v.tensor_mul(c.m_row[0:1, :], c.m_row[0:1, :], c.m_row[0:1, :])
        v.tensor_mul(c.m_row[0:1, :], c.m_row[0:1, :], c.var2_row[0:1, :])
        return v.reduce_max(c.smalls[0:1, c.SM_LNR:c.SM_LNR + 1],
                            c.m_row[0:1, :], axis=AX.X)
    sd.add("vector", v_rows1, extra=[("vst2", 2 * NJT), ("vmrow", NJT)])

    @dma_step
    def y_ar1a(sy):
        return sy.dma_start(out=c.cc1i[:, :],
                            in_=c.smalls[0:1, c.SM_LNR:c.SM_LNR + 1])
    sd.add("sync", y_ar1a)

    gpos1 = sd.mark("gpsimd")

    def g_ar1b(g):
        return g.collective_compute(
            "AllReduce", ALU.max, replica_groups=[list(range(NCORES))],
            ins=[c.cc1i], outs=[c.cc1o])
    sd.add("gpsimd", g_ar1b)
    RLX1 = {"gpsimd": gpos1}

    for wi in range(2):
        m1 = (c.m1g, c.m1u)[wi]
        rs = (c.rsg, c.rsu)[wi]
        for bt in range(6):
            nt = 4 if bt < 5 else 2

            def t_rx(te, m1=m1, bt=bt, nt=nt):
                for q in range(nt):
                    ins = te.transpose(c.pb[6][:, q * 128:(q + 1) * 128],
                                       m1[:, (bt * 4 + q) * 128:(bt * 4 + q + 1) * 128],
                                       c.ident[:, :])
                return ins
            sd.add("tensor", t_rx, relax=RLX1, extra=[("vwp", 64 * (wi + 1))])

            def v_rx(v, rs=rs, bt=bt, nt=nt):
                for q in range(nt):
                    icc = bt * 4 + q
                    ins = v.reduce_max(rs[:, icc:icc + 1],
                                       c.pb[6][:, q * 128:(q + 1) * 128], axis=AX.X)
                return ins
            sd.add("vector", v_rx, relax=RLX1)

    c.mk["rx_done_t"] = sd.mark("tensor")
    p2v = []
    for wi in range(2):
        rs = (c.rsg, c.rsu)[wi]

        sd.add("vector", lambda v, rs=rs: v.tensor_scalar_mul(
            c.invw_col[:, 0:22], rs[:, 0:22], 1.0 / 127.0), relax=RLX1)
        sd.add("vector", lambda v: v.reciprocal(c.invw_col[:, 0:22],
                                                c.invw_col[:, 0:22]), relax=RLX1)

        for bt in range(6):
            nt = 4 if bt < 5 else 2

            def t_tiny(te, bt=bt, nt=nt):
                for q in range(nt):
                    ins = te.transpose(c.pb[6][0:1, q * 128:(q + 1) * 128],
                                       c.invw_col[:, bt * 4 + q:bt * 4 + q + 1],
                                       c.ident[:, :])
                return ins
            sd.add("tensor", t_tiny, relax=RLX1)

            def v_tiny(v, bt=bt, nt=nt):
                for q in range(nt):
                    icc = bt * 4 + q
                    ins = v.tensor_copy(c.winv_row[0:1, icc * 128:(icc + 1) * 128],
                                        c.pb[6][0:1, q * 128:(q + 1) * 128])
                return ins
            sd.add("vector", v_tiny, relax=RLX1)

        for bt in range(0, 6, 2):
            nn2 = 512 if bt + 1 < 5 else 256

            def t_binv(te, bt=bt, nn2=nn2):
                te.matmul(c.pb[6][:, 0:512], c.ones1[:, :],
                          c.winv_row[0:1, bt * 512:(bt + 1) * 512],
                          start=True, stop=True)
                return te.matmul(c.pb[7][:, 0:nn2], c.ones1[:, :],
                                 c.winv_row[0:1, (bt + 1) * 512:(bt + 1) * 512 + nn2],
                                 start=True, stop=True)
            sd.add("tensor", t_binv, relax=RLX1)

            def v_binv(v, bt=bt, nn2=nn2):
                v.tensor_copy(c.binv[:, bt * 512:(bt + 1) * 512],
                              c.pb[6][:, 0:512])
                return v.tensor_copy(
                    c.binv[:, (bt + 1) * 512:(bt + 1) * 512 + nn2],
                    c.pb[7][:, 0:nn2])
            sd.add("vector", v_binv, relax=RLX1)

        qw = (c.qwg, c.qwu)[wi]
        for hh in range(4 * DC):
            hc, hf = divmod(hh, 4)
            n = len(p2v)

            def v_p2(v, qw=qw, hc=hc, hf=hf, b=n % 4):
                wb = c.wp2h[b]
                v.tensor_mul(wb[:, :], wb[:, :],
                             c.binv[:, hf * 704:(hf + 1) * 704])
                return v.tensor_scalar(
                    qw[:, hc * DI + hf * 704:hc * DI + (hf + 1) * 704],
                    wb[:, :], RND, RND, ALU.add, ALU.subtract)
            p2v.append(sd.add("vector", v_p2, relax=RLX1,
                              extra=[(c.dp2[n % 4], 16 * (n // 4 + 1))]))
        if wi == 0:
            c.mk["qwg_done_v"] = sd.mark("vector")
    c.mk["qwu_done_v"] = sd.mark("vector")
    c.p2_v_pos = p2v

    sd.seg = "pre2"

    @dma_step
    def y_ar1c(sy):
        return sy.dma_start(out=c.smalls[0:1, c.SM_LNR:c.SM_LNR + 1],
                            in_=c.cc1o[:, :])
    sd.add("sync", y_ar1c)

    def s_post1(sc):
        sc.activation(c.scal1[0:1, 0:1],
                      c.smalls[0:1, c.SM_LNR:c.SM_LNR + 1], AF.Sqrt)
        return sc.activation(c.var2_row[0:1, :], c.var2_row[0:1, :], AF.Sqrt)
    sd.add("scalar", s_post1)

    def s_post2(sc):
        sc.activation(c.scal1[0:1, 2:3], c.clip_sb[0:1, 0:1], AF.Sigmoid)
        return sc.activation(c.scal1[0:1, 3:4], c.clip_sb[0:1, 1:2], AF.Sigmoid)
    sd.add("scalar", s_post2)

    sd.add("vector", lambda v: v.tensor_mul(c.scal1[0:1, 0:1], c.scal1[0:1, 0:1],
                                            c.scal1[0:1, 2:3]))
    sd.add("vector", lambda v: v.tensor_scalar_mul(c.scal1[0:1, 0:1],
                                                   c.scal1[0:1, 0:1], 1.0 / 127.0))
    sd.add("vector", lambda v: v.reciprocal(c.scal1[0:1, 1:2], c.scal1[0:1, 0:1]))
    sd.add("vector", lambda v: v.tensor_scalar(c.a_row[0:1, :], c.var2_row[0:1, :],
                                               c.scal1[0:1, 1:2], None, ALU.mult))
    sd.add("vector", lambda v: v.tensor_mul(c.b_row[0:1, :], c.mu2_row[0:1, :],
                                            c.a_row[0:1, :]))

    def t_sxcol(te):
        return te.matmul(c.pb[6][:, 0:1], c.ones1[:, :], c.scal1[0:1, 0:1],
                         start=True, stop=True)
    sd.add("tensor", t_sxcol)

    def v_sxcol(v):
        return v.tensor_copy(c.smalls[:, c.SM_SCALEX:c.SM_SCALEX + 1],
                             c.pb[6][:, 0:1])
    sd.add("vector", v_sxcol)

    for rs in (c.rsg, c.rsu):
        sd.add("vector", lambda v, rs=rs: v.tensor_scalar_mul(
            rs[:, 0:22], rs[:, 0:22], 1.0 / 127.0))
        sd.add("vector", lambda v, rs=rs: v.tensor_scalar(
            rs[:, 0:22], rs[:, 0:22],
            c.smalls[:, c.SM_SCALEX:c.SM_SCALEX + 1], None, ALU.mult))

    qxv = []
    for th in range(2):
        for k in range(2):
            row = (c.a_row, c.b_row)[k]
            dst = (c.a_bc, c.b_bc)[k]

            def t_ab(te, row=row, th=th):
                te.matmul(c.pb[6][:, 0:512], c.ones1[:, :],
                          row[0:1, th * 1024:th * 1024 + 512],
                          start=True, stop=True)
                return te.matmul(c.pb[7][:, 0:512], c.ones1[:, :],
                                 row[0:1, th * 1024 + 512:th * 1024 + 1024],
                                 start=True, stop=True)
            sd.add("tensor", t_ab)

            def v_ab(v, dst=dst):
                v.tensor_copy(dst[:, 0:512], c.pb[6][:, 0:512])
                return v.tensor_copy(dst[:, 512:1024], c.pb[7][:, 0:512])
            sd.add("vector", v_ab)
        for hc in range(DC):
            rlx = {}
            if len(qxv) >= 2:
                rlx["vector"] = qxv[-2]

            @dma_step
            def y_qx(sy, th=th, hc=hc):
                return sy.dma_start(out=c.x2re[hc % 2][:, :],
                                    in_=c.x2_sp[hc, :, th * 1024:(th + 1) * 1024])
            sd.add("sync", y_qx, relax=rlx, extra=[("dx2", 16 * DC * NJT)])

            def v_qx(v, th=th, hc=hc):
                xb = c.x2re[hc % 2]
                v.tensor_mul(xb[:, :], xb[:, :], c.a_bc[:, :])
                v.tensor_sub(xb[:, :], xb[:, :], c.b_bc[:, :])
                v.tensor_scalar(xb[:, :], xb[:, :], RND, RND, ALU.add, ALU.subtract)
                return v.tensor_scalar(
                    c.qx[:, hc * TJ + th * 1024:hc * TJ + (th + 1) * 1024],
                    xb[:, :], 127.0, -128.0, ALU.min, ALU.max)
            qxv.append(sd.add("vector", v_qx))
    c.mk["qx_done_v"] = sd.mark("vector")

    sd.seg = "post"
    def t_hmaxT(te):
        return te.transpose(c.pb[6][0:1, 0:128], c.hmax_col[:, :], c.ident[:, :])
    sd.add("tensor", t_hmaxT, extra=[("vh", NICC * 4 + 2)])

    def v_hmax(v):
        return v.reduce_max(c.smalls[0:1, c.SM_HMAX:c.SM_HMAX + 1],
                            c.pb[6][0:1, 0:128], axis=AX.X)
    sd.add("vector", v_hmax)

    @dma_step
    def y_ar2a(sy):
        return sy.dma_start(out=c.cc2i[:, :],
                            in_=c.smalls[0:1, c.SM_HMAX:c.SM_HMAX + 1])
    sd.add("sync", y_ar2a)

    gpos2 = sd.mark("gpsimd")

    def g_ar2b(g):
        return g.collective_compute(
            "AllReduce", ALU.max, replica_groups=[list(range(NCORES))],
            ins=[c.cc2i], outs=[c.cc2o])
    sd.add("gpsimd", g_ar2b)
    RLX2 = {"gpsimd": gpos2}

    for bt in range(2):
        def t_rd(te, bt=bt):
            for q in range(4):
                ins = te.transpose(c.pb[6][:, q * 128:(q + 1) * 128],
                                   c.m1d[:, (bt * 4 + q) * 128:(bt * 4 + q + 1) * 128],
                                   c.ident[:, :])
            return ins
        sd.add("tensor", t_rd, relax=RLX2, extra=[("vwd1", NICC)])

        def v_rd(v, bt=bt):
            for q in range(4):
                ins = v.reduce_max(c.rsd[:, bt * 4 + q:bt * 4 + q + 1],
                                   c.pb[6][:, q * 128:(q + 1) * 128], axis=AX.X)
            return ins
        sd.add("vector", v_rd, relax=RLX2)

    sd.add("vector", lambda v: v.tensor_scalar_mul(
        c.invw_col[:, 0:8], c.rsd[:, 0:8], 1.0 / 127.0), relax=RLX2)
    sd.add("vector", lambda v: v.reciprocal(c.invw_col[:, 0:8],
                                            c.invw_col[:, 0:8]), relax=RLX2)

    for bt in range(2):
        def t_tinyd(te, bt=bt):
            for q in range(4):
                ins = te.transpose(c.pb[6][0:1, q * 128:(q + 1) * 128],
                                   c.invw_col[:, bt * 4 + q:bt * 4 + q + 1],
                                   c.ident[:, :])
            return ins
        sd.add("tensor", t_tinyd, relax=RLX2)

        def v_tinyd(v, bt=bt):
            return v.tensor_copy(c.winv_row[0:1, bt * 512:(bt + 1) * 512],
                                 c.pb[6][0:1, 0:512])
        sd.add("vector", v_tinyd, relax=RLX2)

    for q in range(2):
        def t_binvd(te, q=q):
            return te.matmul(c.pb[7][:, 0:512], c.ones1[:, :],
                             c.winv_row[0:1, q * 512:(q + 1) * 512],
                             start=True, stop=True)
        sd.add("tensor", t_binvd, relax=RLX2)

        def v_binvd(v, q=q):
            return v.tensor_copy(c.binvd[:, q * 512:(q + 1) * 512], c.pb[7][:, 0:512])
        sd.add("vector", v_binvd, relax=RLX2)

    wd2v = []
    for icc in range(NICC):
        def v_wd2(v, icc=icc):
            wb = c.wdbuf[icc % 3]
            v.tensor_mul(wb[:, :], wb[:, :], c.binvd[:, :])
            return v.tensor_scalar(c.qwd[:, icc * D:(icc + 1) * D], wb[:, :],
                                   RND, RND, ALU.add, ALU.subtract)
        wd2v.append(sd.add("vector", v_wd2, relax=RLX2,
                           extra=[(c.dwd2[icc % 3], 16 * (icc // 3 + 1))]))
    c.wd2_v_pos = wd2v

    sd.seg = "post2"

    @dma_step
    def y_ar2c(sy):
        return sy.dma_start(out=c.smalls[0:1, c.SM_HMAX:c.SM_HMAX + 1],
                            in_=c.cc2o[:, :])
    sd.add("sync", y_ar2c)

    sd.add("vector", lambda v: v.tensor_mul(c.scal1[0:1, 5:6], c.scal1[0:1, 3:4],
                                            c.smalls[0:1, c.SM_HMAX:c.SM_HMAX + 1]))
    sd.add("vector", lambda v: v.tensor_scalar_mul(c.scal1[0:1, 5:6],
                                                   c.scal1[0:1, 5:6], 1.0 / 127.0))
    sd.add("vector", lambda v: v.reciprocal(c.scal1[0:1, 6:7], c.scal1[0:1, 5:6]))

    def t_sdcol(te):
        te.matmul(c.pb[6][:, 0:1], c.ones1[:, :], c.scal1[0:1, 5:6],
                  start=True, stop=True)
        return te.matmul(c.pb[6][:, 4:5], c.ones1[:, :], c.scal1[0:1, 6:7],
                         start=True, stop=True)
    sd.add("tensor", t_sdcol)

    def v_sdcol(v):
        v.tensor_copy(c.smalls[:, c.SM_SCALED:c.SM_SCALED + 1], c.pb[6][:, 0:1])
        return v.tensor_copy(c.smalls[:, c.SM_INVSD:c.SM_INVSD + 1], c.pb[6][:, 4:5])
    sd.add("vector", v_sdcol)

    sd.add("vector", lambda v: v.tensor_scalar_mul(c.rsd[:, 0:8], c.rsd[:, 0:8],
                                                   1.0 / 127.0))
    sd.add("vector", lambda v: v.tensor_scalar(
        c.rsd[:, 0:8], c.rsd[:, 0:8],
        c.smalls[:, c.SM_SCALED:c.SM_SCALED + 1], None, ALU.mult))

    qhv = []
    for n in range(2 * NICC):
        icc, hf = divmod(n, 2)
        on_scalar = (n % 3 != 0)

        if on_scalar:
            def s_qh(sc, b=n % 4):
                hb = c.hre[b]
                sc.activation(hb[:, :], hb[:, :], AF.Identity,
                              bias=c.smalls[:, 12:13],
                              scale=c.smalls[:, c.SM_INVSD:c.SM_INVSD + 1])
                return sc.activation(hb[:, :], hb[:, :], AF.Identity,
                                     bias=c.smalls[:, 13:14])
            sd.add("scalar", s_qh, extra=[(c.dqh[n % 4], 16 * (n // 4 + 1))])

            def v_qh(v, icc=icc, hf=hf, b=n % 4):
                return v.tensor_scalar(
                    c.qh_ap(icc)[:, hf * 1024:(hf + 1) * 1024], c.hre[b][:, :],
                    127.0, -128.0, ALU.min, ALU.max)
            qhv.append(sd.add("vector", v_qh))
        else:
            def v_qh(v, icc=icc, hf=hf, b=n % 4):
                hb = c.hre[b]
                v.tensor_scalar(hb[:, :], hb[:, :],
                                c.smalls[:, c.SM_INVSD:c.SM_INVSD + 1],
                                RND, ALU.mult, ALU.add)
                v.tensor_scalar_sub(hb[:, :], hb[:, :], RND)
                return v.tensor_scalar(
                    c.qh_ap(icc)[:, hf * 1024:(hf + 1) * 1024], hb[:, :],
                    127.0, -128.0, ALU.min, ALU.max)
            qhv.append(sd.add("vector", v_qh,
                              extra=[(c.dqh[n % 4], 16 * (n // 4 + 1))]))
    c.mk["qh_done_v"] = sd.mark("vector")
    c.qh_v_pos = qhv
    c.mk["qwd_done_v"] = sd.mark("vector")
    c.mk["f_done_t"] = sd.mark("tensor")



def _emit_gpsimd(c, block):
    s = c.s

    @block.gpsimd
    def _(g):
        s["gi"].inc(g.memset(c.ident[:, :], 0.0), 1)
        g.wait_ge(s["gi"].h, 1)
        g.affine_select(out=c.ident[:, :], in_=c.ident[:, :],
                        compare_op=ALU.not_equal, fill=1.0, base=0,
                        pattern=[[-1, 128]], channel_multiplier=1)
        g.memset(c.ones1[:, :], 1.0)
        g.memset(c.ones1b[:, :], 1.0)
        g.memset(c.onesc[:, :], 1.0)
        g.memset(c.cexp_s[:, :], 1.0 / 32.0)
        g.memset(c.smalls[:, 0:12], 0.0)
        g.memset(c.smalls[:, 14:16], 0.0)
        g.memset(c.smalls[:, 12:13], RND)
        g.memset(c.smalls[:, 13:14], -RND)
        g.memset(c.scal1[:, :], 0.0)
        s["gi"].inc(g.memset(c.cexp_b[:, :], -3.0), 1)
        s["dg"].inc(g.dma_start(out=c.clip_sb[:, :], in_=c.clip[:, :]), 16)
        for wsb, wr in ((c.wq_sb, c.wq_r), (c.wk_sb, c.wk_r), (c.wv_sb, c.wv_r)):
            for dc in range(DC):
                s["dg"].inc(g.dma_start(out=wsb[:, dc * D:(dc + 1) * D],
                                        in_=wr[dc]), 16)
        for tt in range(NTT):
            if tt >= 3:
                g.wait_ge(s["sev"].h, _a_group_count(tt - 2))
                g.wait_ge(c.dxa[tt % 3].h, 128 * (tt // 3))
            for dc in range(DC):
                c.dxa[tt % 3].inc(g.dma_start(
                    out=c.bufx[tt % 3][:, dc * 512:(dc + 1) * 512],
                    in_=c.xT_r[dc, :, tt * 512:(tt + 1) * 512]), 16)
        if c.stage < 2:
            return
        g.wait_ge(s["tqkv"].h, NTT)
        g.wait_ge(s["sev"].h, A_GROUPS_TOTAL)
        for jt in range(NJT):
            for ic in range(NIC):
                pi = jt * NIC + ic
                if pi >= 4:
                    g.wait_ge(s["tAV"].h, pi - 3)
                    g.wait_ge(c.dvb[pi % 4].h, 16 * (pi // 4))
                c.dvb[pi % 4].inc(g.dma_start(
                    out=c.vbuf[pi % 4][:, :],
                    in_=c.v_sp[ic * 128:(ic + 1) * 128, :]), 16)
        if c.stage < 3:
            return
        c.sched.replay("gpsimd", g, seg="pre")
        c.sched.replay("gpsimd", g, seg="pre2")
        c.sched.replay("gpsimd", g, seg="post")
        g.wait_ge(c.dhsp[0].h, 16 * NICC * 2)
        g.wait_ge(c.dhsp[1].h, 16 * NICC * 2)
        for n in range(1, 2 * NICC, 2):
            icc, hf = divmod(n, 2)
            if n >= 4:
                g.wait_ge(c.chain["vector"].h, c.qh_v_pos[n - 4])
            c.dqh[n % 4].inc(g.dma_start(
                out=c.hre[n % 4][:, :],
                in_=c.h_sp[icc, :, hf * 1024:(hf + 1) * 1024]), 16)
        c.sched.replay("gpsimd", g, seg="post2")



def _emit_sync(c, block):
    s = c.s

    @block.sync
    def _(sy):
        for tc in range(32):
            sy.wait_ge(s["sev"].h, _vev_count(tc))
            if tc >= 2:
                sy.wait_ge(c.dvsp[tc % 2].h, 16 * (tc // 2))
            c.dvsp[tc % 2].inc(sy.dma_start(
                out=c.v_sp[tc * 128:(tc + 1) * 128, :],
                in_=c.vstage[tc % 2][:, :]), 16)
        if c.stage < 2:
            if c.taps:
                _emit_taps_a(c, sy)
            return
        if c.taps:
            sy.wait_ge(c.dvsp[0].h, 16 * 16)
            sy.wait_ge(c.dvsp[1].h, 16 * 16)
            s["dout2"].inc(sy.dma_start(out=c.dbg_vst[:, :],
                                        in_=c.vstage[1][:, :]), 16)
        sy.wait_ge(s["tqkv"].h, NTT)
        nqs = 0

        def _y_quarters(budget):
            nonlocal nqs
            while nqs < min(128, budget):
                if nqs >= 4:
                    sy.wait_ge(s["vwp"].h, nqs - 3)
                nq, hf = divmod(nqs, 2)
                wr = c.wgT_r if nq < 32 else c.wuT_r
                qq = nq % 32
                off = (qq % 4) * 704 + hf * 352
                c.dwp[nqs % 4].inc(sy.dma_start(
                    out=c.wp1[nqs % 4][:, :],
                    in_=wr[qq // 4, :, off:off + 352]), 16)
                nqs += 1

        for jt in range(NJT):
            _y_quarters(16 * jt + 4)
            if jt >= 1:
                sy.wait_ge(s["vOev"].h, jt)
            if jt >= 2:
                sy.wait_ge(c.dxr[jt % 2].h, 16 * DC * (jt // 2))
            for ec in range(DC):
                c.dxr[jt % 2].inc(sy.dma_start(
                    out=c.xresb[:, ec * 256:(ec + 1) * 256],
                    in_=c.xT_r[ec, :, jt * 256:(jt + 1) * 256]), 16)
            sy.wait_ge(s["vOev"].h, jt + 1)
            sy.wait_ge(s["dx2"].h, 16 * DC * jt)
            for ec in range(DC):
                s["dx2"].inc(sy.dma_start(
                    out=c.x2_sp[ec, :, jt * 256:(jt + 1) * 256],
                    in_=c.x2stage[:, ec * 256:(ec + 1) * 256]), 16)
        _y_quarters(128)
        if c.stage < 3:
            if c.taps:
                _emit_taps_b(c, sy)
            return
        c.sched.replay("sync", sy, seg="pre")
        sy.wait_ge(c.chain["tensor"].h, c.mk["rx_done_t"])
        for n in range(0, 64, 2):
            wi, q = divmod(n, 32)
            hc, hf = q // 4, q % 4
            wr = (c.wgT_r, c.wuT_r)[wi]
            if n >= 4:
                sy.wait_ge(c.chain["vector"].h, c.p2_v_pos[n - 4])
            c.dp2[n % 4].inc(sy.dma_start(
                out=c.wp2h[n % 4][:, :],
                in_=wr[hc, :, hf * 704:(hf + 1) * 704]), 16)
        c.sched.replay("sync", sy, seg="pre2")
        sy.wait_ge(c.chain["vector"].h, c.mk["qwu_done_v"])
        for it in range(NICC * 4):
            if it % 4 == 0:
                icc = it // 4
                if icc >= 2:
                    sy.wait_ge(s["vwd1"].h, icc - 1)
                    sy.wait_ge(c.dwd1[icc % 2].h, 16 * (icc // 2))
                c.dwd1[icc % 2].inc(sy.dma_start(out=c.wdbuf[icc % 2][:, :],
                                                 in_=c.wdT_r[icc]), 16)
            sy.wait_ge(s["vh"].h, it + 1)
            if it >= 2:
                sy.wait_ge(c.dhsp[it % 2].h, 16 * (it // 2))
            c.dhsp[it % 2].inc(sy.dma_start(
                out=c.h_sp[it // 4, :, (it % 4) * 512:(it % 4 + 1) * 512],
                in_=c.htile[it % 2][:, :]), 16)
        c.sched.replay("sync", sy, seg="post")
        sy.wait_ge(s["vwd1"].h, NICC)
        for icc in range(0, NICC, 2):
            if icc >= 3:
                sy.wait_ge(c.chain["vector"].h, c.wd2_v_pos[icc - 3])
            c.dwd2[icc % 3].inc(sy.dma_start(out=c.wdbuf[icc % 3][:, :],
                                             in_=c.wdT_r[icc]), 16)
        sy.wait_ge(c.dhsp[0].h, 16 * NICC * 2)
        sy.wait_ge(c.dhsp[1].h, 16 * NICC * 2)

        def _y_qh(n):
            icc, hf = divmod(n, 2)
            if n == 0:
                sy.wait_ge(c.chain["vector"].h, c.wd2_v_pos[-1])
            if n >= 4:
                sy.wait_ge(c.chain["vector"].h, c.qh_v_pos[n - 4])
            c.dqh[n % 4].inc(sy.dma_start(
                out=c.hre[n % 4][:, :],
                in_=c.h_sp[icc, :, hf * 1024:(hf + 1) * 1024]), 16)

        _y_qh(0)
        _y_qh(2)
        c.sched.replay("sync", sy, seg="post2")
        for n in range(4, 2 * NICC, 2):
            _y_qh(n)
        sy.wait_ge(c.chain["vector"].h, c.mk["qh_done_v"])
        for hc in range(DC):
            if hc >= 2:
                sy.wait_ge(s["vdn"].h, (hc - 1) * 4)
                sy.wait_ge(c.dx2b[hc % 2].h, 16 * (hc // 2))
            c.dx2b[hc % 2].inc(sy.dma_start(out=c.x2re2[hc % 2][:, :],
                                            in_=c.x2_sp[hc]), 16)
            sy.wait_ge(s["vdn"].h, (hc + 1) * 4)
            if hc >= 2:
                sy.wait_ge(c.dout[hc % 2].h, 16 * (hc // 2))
            c.dout[hc % 2].inc(sy.dma_start(
                out=c.outT[hc * 128:(hc + 1) * 128, :],
                in_=c.ostage[hc % 2][:, :]), 16)
        if c.taps:
            s["dout2"].inc(sy.dma_start(out=c.dbg_sc[:, 0:16], in_=c.smalls[:, :]), 16)
            s["dout2"].inc(sy.dma_start(out=c.dbg_sc[:, 16:38], in_=c.rsg[:, :]), 16)
            s["dout2"].inc(sy.dma_start(out=c.dbg_sc[:, 38:60], in_=c.rsu[:, :]), 16)
            s["dout2"].inc(sy.dma_start(out=c.dbg_sc[:, 60:68], in_=c.rsd[:, :]), 16)
            s["dout2"].inc(sy.dma_start(out=c.dbg_sc[0:1, 68:76], in_=c.scal1[0:1, :]), 16)
            sy.wait_ge(s["dout2"].h, s["dout2"].n)
        sy.wait_ge(c.dout[0].h, 16 * 4)
        sy.wait_ge(c.dout[1].h, 16 * 4)


def _emit_taps_a(c, sy):
    s = c.s
    sy.wait_ge(s["tqkv"].h, NTT)
    sy.wait_ge(s["sev"].h, A_GROUPS_TOTAL)
    n = 0
    for dst, src in ((c.dbg_q, c.q_sb), (c.dbg_k, c.k_sb), (c.dbg_mup, c.mup_row)):
        s["dout2"].inc(sy.dma_start(out=dst[:, :], in_=src[:, :]), 16)
        n += 16
    sy.wait_ge(s["dout2"].h, n)
    sy.wait_ge(c.dvsp[0].h, 16 * 16)
    sy.wait_ge(c.dvsp[1].h, 16 * 16)


def _emit_taps_b(c, sy):
    s = c.s
    sy.wait_ge(s["vst2"].h, NJT)
    sy.wait_ge(s["vmrow"].h, NJT)
    n = 0
    for i, row in enumerate((c.mu2_row, c.var2_row, c.m_row)):
        s["dout2"].inc(sy.dma_start(out=c.dbg_rows[:, i * TJ:(i + 1) * TJ],
                                    in_=row[:, :]), 16)
        n += 16
    sy.wait_ge(s["dout2"].h, n)
    sy.wait_ge(s["dx2"].h, 16 * DC * NJT)
    sy.wait_ge(c.dvsp[0].h, 16 * 16)
    sy.wait_ge(c.dvsp[1].h, 16 * 16)



def _emit_tensor(c, block):
    s = c.s

    @block.tensor
    def _(te):
        te.wait_ge(s["gi"].h, 2)
        te.wait_ge(s["dg"].h, 16 * 25)
        def _stats_x(tt):
            te.wait_ge(c.dxa[tt % 3].h, 128 * (tt // 3 + 1))
            if tt >= 1:
                te.wait_ge(s["vstat"].h, tt)
            ins = None
            for dc in range(DC):
                ins = te.matmul(c.pb[2][0:1, 0:512], c.onesc[:, :],
                                c.bufx[tt % 3][:, dc * 512:(dc + 1) * 512],
                                start=(dc == 0), stop=(dc == DC - 1))
            s["tsx"].inc(ins, 1)
            te.wait_ge(s["tsx"].h, s["tsx"].n)

        def _stats_sq(tt):
            te.wait_ge(s["ssq"].h, tt + 1)
            ins = None
            for dc in range(DC):
                xs = c.xsqA if dc < 4 else c.xsqB
                ins = te.matmul(c.pb[3][0:1, 0:512], c.onesc[:, :],
                                xs[:, (dc % 4) * 512:(dc % 4 + 1) * 512],
                                start=(dc == 0), stop=(dc == DC - 1))
            s["tstat"].inc(ins, 1)
            te.wait_ge(s["tstat"].h, s["tstat"].n)

        def _bcast(tt):
            te.wait_ge(s["sinv"].h, tt + 1)
            if tt >= 1:
                te.wait_ge(s["vbc"].h, tt)
            te.matmul(c.pb[7][:, 0:512], c.ones1[:, :],
                      c.scr_tmp[0:1, 0:512], start=True, stop=True)
            te.wait_ge(s["vmup"].h, tt + 1)
            ins = te.matmul(c.pb[6][:, 0:512], c.ones1b[:, :],
                            c.mup_row[0:1, tt * 512:(tt + 1) * 512],
                            start=True, stop=True)
            s["tbc"].inc(ins, 1)
            te.wait_ge(s["tbc"].h, s["tbc"].n)

        def _qkv(tt, mids):
            te.wait_ge(s["vxp"].h, tt + 1)
            gbase = _a_group_count(tt)
            for gi_, grp in enumerate(_a_groups(tt)):
                if gi_ in mids:
                    mids[gi_]()
                g = gbase + gi_
                bank = c.pb[(0, 1, 4, 5)[g % 4]]
                if g >= 4:
                    te.wait_ge(s["sev"].h, g - 3)
                kind = grp[0]
                if kind in ("q", "k"):
                    ec = grp[1]
                    wsb = c.wq_sb if kind == "q" else c.wk_sb
                    for dc in range(DC):
                        ins = te.matmul(bank[:, 0:512],
                                        wsb[:, dc * D + ec * 128:dc * D + (ec + 1) * 128],
                                        c.bufx[tt % 3][:, dc * 512:(dc + 1) * 512],
                                        start=(dc == 0), stop=(dc == DC - 1))
                else:
                    tc, eh = grp[1], grp[2]
                    for dc in range(DC):
                        ins = te.matmul(
                            bank[:, 0:512],
                            c.bufx[tt % 3][:, dc * 512 + (tc % 4) * 128:dc * 512 + (tc % 4 + 1) * 128],
                            c.wv_sb[:, dc * D + eh * 512:dc * D + (eh + 1) * 512],
                            start=(dc == 0), stop=(dc == DC - 1))
                s["tqg"].inc(ins, 1)
            s["tqkv"].inc(te.nop(), 1)

        for tt in range(NTT):
            _stats_x(tt)
            if tt == 0:
                _stats_sq(0)
                _bcast(0)
            else:
                nb = 10 if len(_a_groups(tt - 1)) == 24 else 8
                _qkv(tt - 1, {4: (lambda t=tt: _stats_sq(t)),
                              nb: (lambda t=tt: _bcast(t))})
        _qkv(NTT - 1, {})
        if c.stage < 2:
            return

        te.wait_ge(s["vstat"].h, NTT)
        te.wait_ge(s["sev"].h, A_GROUPS_TOTAL)
        for jt in range(NJT):
            epil = _b_epilogue_steps(c, jt - 1)
            epil_at = {8: 0, 12: 1, 20: 2, 26: 3}
            for ic in range(NIC):
                gS = jt * NIC + ic
                if gS >= 2:
                    te.wait_ge(s["sE"].h, gS - 1)
                ins = None
                for ec in range(DC):
                    ins = te.matmul(
                        c.pb[ic % 2][:, 0:256],
                        c.q_sb[:, ec * N + ic * 128:ec * N + (ic + 1) * 128],
                        c.k_sb[:, ec * TJ + jt * 256:ec * TJ + (jt + 1) * 256],
                        start=(ec == 0), stop=(ec == DC - 1))
                s["tS"].inc(ins, 1)
                if ic in epil_at and epil_at[ic] < len(epil):
                    epil[epil_at[ic]](te)
                if ic >= 2:
                    _b_av(c, te, jt, ic - 2)
            _b_av(c, te, jt, NIC - 2)
            _b_av(c, te, jt, NIC - 1)
            s["trr"].inc(te.nop(), 1)
            te.wait_ge(s["vrv"].h, jt + 1)
            if jt >= 1:
                te.wait_ge(s["vmrow"].h, jt)
            for q in range(2):
                ins = te.transpose(c.pb[7][0:1, q * 128:(q + 1) * 128],
                                   c.smalls[:, 14 + q:15 + q], c.ident[:, :])
            s["trt"].inc(ins, 1)
            te.wait_ge(s["trt"].h, s["trt"].n)
            te.wait_ge(s["vrt"].h, jt + 1)
            ins = te.matmul(c.pb[7][:, 0:256], c.ones1[:, :],
                            c.rinv_row[0:1, 0:256], start=True, stop=True)
            s["tb7"].inc(ins, 1)
        for step in _b_epilogue_steps(c, NJT - 1):
            step(te)
        if c.stage < 3:
            return

        c.sched.replay("tensor", te, seg="pre")
        c.sched.replay("tensor", te, seg="pre2")
        vch = c.chain["vector"]
        for icc in range(NICC):
            for tp in range(2):
                pi = icc * 2 + tp
                t4a, t4b = 2 * tp, 2 * tp + 1
                bga = c.pb[(pi % 2) * 4]
                bgb = c.pb[(pi % 2) * 4 + 1]
                bua = c.pb[(pi % 2) * 4 + 2]
                bub = c.pb[(pi % 2) * 4 + 3]
                if pi == 0:
                    te.wait_ge(vch.h, c.mk["qx_done_v"])
                if pi >= 2:
                    te.wait_ge(s["sgu"].h, (2 * pi - 2) * 2)
                for hc in range(DC):
                    wsl = c.qwg[:, hc * DI + icc * 128:hc * DI + (icc + 1) * 128]
                    te.matmul(bga[:, 0:512], wsl,
                              c.qx[:, hc * TJ + t4a * 512:hc * TJ + (t4a + 1) * 512],
                              start=(hc == 0), stop=(hc == DC - 1))
                    te.matmul(bgb[:, 0:512], wsl,
                              c.qx[:, hc * TJ + t4b * 512:hc * TJ + (t4b + 1) * 512],
                              start=(hc == 0), stop=(hc == DC - 1))
                for hc in range(DC):
                    wsl = c.qwu[:, hc * DI + icc * 128:hc * DI + (icc + 1) * 128]
                    insa = te.matmul(
                        bua[:, 0:512], wsl,
                        c.qx[:, hc * TJ + t4a * 512:hc * TJ + (t4a + 1) * 512],
                        start=(hc == 0), stop=(hc == DC - 1))
                    insb = te.matmul(
                        bub[:, 0:512], wsl,
                        c.qx[:, hc * TJ + t4b * 512:hc * TJ + (t4b + 1) * 512],
                        start=(hc == 0), stop=(hc == DC - 1))
                s["tgu"].inc(insa, 1)
                s["tgu"].inc(insb, 1)
        c.sched.replay("tensor", te, seg="post")
        c.sched.replay("tensor", te, seg="post2")
        for hc in range(DC):
            if hc == 0:
                te.wait_ge(vch.h, c.mk["qh_done_v"])
            if hc >= 2:
                te.wait_ge(s["sdn"].h, (hc - 1) * 4)
            ins = None
            for icc in range(NICC):
                for t4 in range(4):
                    ins = te.matmul(
                        c.pb[(hc % 2) * 4 + t4][:, 0:512],
                        c.qwd[:, icc * D + hc * 128:icc * D + (hc + 1) * 128],
                        c.qh_ap(icc)[:, t4 * 512:(t4 + 1) * 512],
                        start=(icc == 0), stop=(icc == NICC - 1))
            s["tdn"].inc(ins, 1)


def _b_av(c, te, jt, ic):
    s = c.s
    gAV = jt * NIC + ic
    te.wait_ge(s["sE"].h, gAV + 1)
    te.wait_ge(c.dvb[gAV % 4].h, 16 * (gAV // 4 + 1))
    if ic == 0:
        te.wait_ge(s["vOev"].h, jt)
        te.wait_ge(s["vrv"].h, jt)
    for ec in range(DC):
        te.matmul(c.pb[2 + ec // 2][:, (ec % 2) * 256:(ec % 2) * 256 + 256],
                  c.vbuf[gAV % 4][:, ec * 128:(ec + 1) * 128],
                  c.E[:, ic * 256:(ic + 1) * 256],
                  start=(ic == 0 and ec % 2 == 0),
                  stop=(ic == NIC - 1 and ec % 2 == 1),
                  skip_group_check=True)
    for jc in range(2):
        ins = te.matmul(c.pb[6][:, jc:jc + 1],
                        c.E[:, ic * 256 + jc * 128:ic * 256 + (jc + 1) * 128],
                        c.onesc[:, :],
                        start=(ic == 0 and jc == 0), stop=(ic == NIC - 1 and jc == 1),
                        skip_group_check=True)
    s["tAV"].inc(ins, 1)


def _b_epilogue_steps(c, jt):
    s = c.s
    if jt < 0:
        return []

    def st_stats_a(te):
        te.wait_ge(s["sx2a"].h, jt + 1)
        te.wait_ge(s["vb7"].h, jt + 1)
        ins = None
        for ec in range(DC):
            ins = te.matmul(c.pb[7][0:1, 256:512], c.onesc[:, :],
                            c.x2b[:, ec * 256:(ec + 1) * 256],
                            start=(ec == 0), stop=(ec == DC - 1))
        s["tst2a"].inc(ins, 1)
        te.wait_ge(s["tst2a"].h, s["tst2a"].n)

    def st_stats_b(te):
        te.wait_ge(s["sx2b"].h, jt + 1)
        te.wait_ge(s["vst2"].h, 2 * jt + 1)
        ins = None
        for ec in range(DC):
            ins = te.matmul(c.pb[7][0:1, 0:256], c.onesc[:, :],
                            c.x2b[:, ec * 256:(ec + 1) * 256],
                            start=(ec == 0), stop=(ec == DC - 1))
        s["tst2b"].inc(ins, 1)
        te.wait_ge(s["tst2b"].h, s["tst2b"].n)

    def st_mt_tp(te):
        te.wait_ge(s["vmt"].h, jt + 1)
        te.wait_ge(s["vst2"].h, 2 * jt + 2)
        for q in range(2):
            te.transpose(c.pb[7][:, q * 128:(q + 1) * 128],
                         c.Mt[:, q * 128:(q + 1) * 128], c.ident[:, :])
        ins = None
        for q in range(2):
            ins = te.transpose(c.pb[7][:, 256 + q * 128:256 + (q + 1) * 128],
                               c.mt[:, q * 128:(q + 1) * 128], c.ident[:, :])
        s["tmt"].inc(ins, 1)

    def st_mc_tp(te):
        te.wait_ge(s["vmc"].h, jt + 1)
        ins = None
        for q in range(2):
            te.transpose(c.pb[7][0:1, q * 128:(q + 1) * 128],
                         c.smalls[:, c.SM_MC + q:c.SM_MC + q + 1], c.ident[:, :])
        for q in range(2):
            ins = te.transpose(c.pb[7][0:1, 256 + q * 128:256 + (q + 1) * 128],
                               c.smalls[:, c.SM_MC + 2 + q:c.SM_MC + 3 + q],
                               c.ident[:, :])
        s["tmt"].inc(ins, 1)
        te.wait_ge(s["tmt"].h, s["tmt"].n)

    return [st_stats_a, st_stats_b, st_mt_tp, st_mc_tp]



def _emit_scalar(c, block):
    s = c.s

    @block.scalar
    def _(sc):
        def _sq(tt):
            sc.wait_ge(c.dxa[tt % 3].h, 128 * (tt // 3 + 1))
            if tt >= 1:
                sc.wait_ge(s["tstat"].h, tt)
            ins = None
            for dc in range(DC):
                xs = c.xsqA if dc < 4 else c.xsqB
                ins = sc.activation(xs[:, (dc % 4) * 512:(dc % 4 + 1) * 512],
                                    c.bufx[tt % 3][:, dc * 512:(dc + 1) * 512],
                                    AF.Square)
            s["ssq"].inc(ins, 1)

        def _sqrt(tt):
            sc.wait_ge(s["vstat"].h, tt + 1)
            ins = sc.activation(c.scr_tmp[0:1, 0:512], c.scr_var[0:1, 0:512], AF.Sqrt)
            s["sinv"].inc(ins, 1)

        def _evacs(tt, lo=0, hi=None):
            groups = _a_groups(tt)
            gbase = _a_group_count(tt)
            for gi_ in range(lo, len(groups) if hi is None else hi):
                grp = groups[gi_]
                g = gbase + gi_
                bank = c.pb[(0, 1, 4, 5)[g % 4]]
                sc.wait_ge(s["tqg"].h, g + 1)
                kind = grp[0]
                if kind == "q":
                    ec = grp[1]
                    dst = c.q_sb[:, ec * N + tt * 512:ec * N + (tt + 1) * 512]
                elif kind == "k":
                    ec = grp[1]
                    dst = c.k_sb[:, ec * TJ + tt * 512:ec * TJ + (tt + 1) * 512]
                else:
                    tc, eh = grp[1], grp[2]
                    if eh == 0 and tc >= 2:
                        sc.wait_ge(c.dvsp[tc % 2].h, 16 * (tc // 2))
                    dst = c.vstage[tc % 2][:, eh * 512:(eh + 1) * 512]
                ins = sc.activation(dst, bank[:, 0:512], AF.Identity)
                s["sev"].inc(ins, 1)

        _sq(0)
        _sq(1)
        _sqrt(0)
        for k in range(1, NTT):
            _evacs(k - 1, 0, 5)
            _sqrt(k)
            _evacs(k - 1, 5, 13)
            if k + 1 < NTT:
                _sq(k + 1)
            _evacs(k - 1, 13)
        _evacs(NTT - 1)
        if c.stage < 2:
            return

        for jt in range(NJT):
            for ic in range(NIC):
                if jt >= 1 and ic == 2:
                    sc.wait_ge(s["vOev"].h, jt)
                    ins = sc.activation(c.x2b[:, :], c.x2stage[:, :], AF.Identity)
                    s["sx2a"].inc(ins, 1)
                if jt >= 1 and ic == 10:
                    sc.wait_ge(s["tst2a"].h, jt)
                    ins = sc.activation(c.x2b[:, :], c.x2b[:, :], AF.Square)
                    s["sx2b"].inc(ins, 1)
                gS = jt * NIC + ic
                sc.wait_ge(s["tS"].h, gS + 1)
                ins = sc.activation(c.E[:, ic * 256:(ic + 1) * 256],
                                    c.pb[ic % 2][:, 0:256], AF.Exp,
                                    bias=c.cexp_b[:, :], scale=c.cexp_s[:, :])
                s["sE"].inc(ins, 1)
        sc.wait_ge(s["vOev"].h, NJT)
        ins = sc.activation(c.x2b[:, :], c.x2stage[:, :], AF.Identity)
        s["sx2a"].inc(ins, 1)
        sc.wait_ge(s["tst2a"].h, NJT)
        ins = sc.activation(c.x2b[:, :], c.x2b[:, :], AF.Square)
        s["sx2b"].inc(ins, 1)
        if c.stage < 3:
            return

        c.sched.replay("scalar", sc, seg="pre")
        sc.wait_ge(c.chain["tensor"].h, c.mk["rx_done_t"])
        for n in range(1, 64, 2):
            wi, q = divmod(n, 32)
            hc, hf = q // 4, q % 4
            wr = (c.wgT_r, c.wuT_r)[wi]
            if n >= 4:
                sc.wait_ge(c.chain["vector"].h, c.p2_v_pos[n - 4])
            c.dp2[n % 4].inc(sc.dma_start(
                out=c.wp2h[n % 4][:, :],
                in_=wr[hc, :, hf * 704:(hf + 1) * 704]), 16)
        c.sched.replay("scalar", sc, seg="pre2")
        for icc in range(NICC):
            for t4 in range(4):
                it = icc * 4 + t4
                pi = icc * 2 + t4 // 2
                bg = c.pb[(pi % 2) * 4 + (t4 % 2)]
                bu = c.pb[(pi % 2) * 4 + 2 + (t4 % 2)]
                sc.wait_ge(s["tgu"].h, it + 1)
                if it >= 1:
                    sc.wait_ge(s["vh"].h, it)
                sc.activation(c.sg[:, :], bg[:, 0:512], AF.Silu,
                              scale=c.rsg[:, icc:icc + 1])
                ins = sc.activation(c.su[:, :], bu[:, 0:512], AF.Identity,
                                    scale=c.rsu[:, icc:icc + 1])
                s["sgu"].inc(ins, 2)
        c.sched.replay("scalar", sc, seg="post")
        sc.wait_ge(s["vwd1"].h, NICC)
        for icc in range(1, NICC, 2):
            if icc >= 3:
                sc.wait_ge(c.chain["vector"].h, c.wd2_v_pos[icc - 3])
            c.dwd2[icc % 3].inc(sc.dma_start(out=c.wdbuf[icc % 3][:, :],
                                             in_=c.wdT_r[icc]), 16)
        c.sched.replay("scalar", sc, seg="post2")
        for hc in range(DC):
            sc.wait_ge(s["tdn"].h, hc + 1)
            if hc >= 2:
                sc.wait_ge(c.dout[hc % 2].h, 16 * (hc // 2))
            for t4 in range(4):
                ins = sc.activation(
                    c.ostage[hc % 2][:, t4 * 512:(t4 + 1) * 512],
                    c.pb[(hc % 2) * 4 + t4][:, 0:512], AF.Identity,
                    scale=c.rsd[:, hc:hc + 1])
                s["sdn"].inc(ins, 1)



def _emit_vector(c, block):
    s = c.s

    @block.vector
    def _(v):
        for tt in range(NTT):
            v.wait_ge(s["tstat"].h, tt + 1)
            if tt >= 1:
                v.wait_ge(s["tbc"].h, tt)
                v.wait_ge(s["vmup"].h, tt)
            sb0, sb1 = c.pb[2], c.pb[3]
            v.tensor_scalar_mul(c.scr_mu[0:1, 0:512], sb0[0:1, 0:512], 1.0 / D)
            v.tensor_scalar_mul(c.scr_var[0:1, 0:512], sb1[0:1, 0:512], 1.0 / D)
            v.tensor_mul(c.scr_tmp[0:1, 0:512], c.scr_mu[0:1, 0:512],
                         c.scr_mu[0:1, 0:512])
            v.tensor_sub(c.scr_var[0:1, 0:512], c.scr_var[0:1, 0:512],
                         c.scr_tmp[0:1, 0:512])
            v.tensor_scalar_add(c.scr_var[0:1, 0:512], c.scr_var[0:1, 0:512], EPS)
            ins = v.reciprocal(c.scr_var[0:1, 0:512], c.scr_var[0:1, 0:512])
            s["vstat"].inc(ins, 1)
            v.wait_ge(s["sinv"].h, tt + 1)
            ins = v.tensor_mul(c.mup_row[0:1, tt * 512:(tt + 1) * 512],
                               c.scr_mu[0:1, 0:512], c.scr_tmp[0:1, 0:512])
            s["vmup"].inc(ins, 1)
            v.wait_ge(s["tbc"].h, tt + 1)
            v.tensor_copy(c.invsig_bc[:, :], c.pb[7][:, 0:512])
            ins = v.tensor_copy(c.mup_bc[:, :], c.pb[6][:, 0:512])
            s["vbc"].inc(ins, 1)
            ins = None
            for dc in range(DC):
                sl = slice(dc * 512, (dc + 1) * 512)
                v.tensor_mul(c.bufx[tt % 3][:, sl], c.bufx[tt % 3][:, sl],
                             c.invsig_bc[:, :])
                ins = v.tensor_sub(c.bufx[tt % 3][:, sl], c.bufx[tt % 3][:, sl],
                                   c.mup_bc[:, :])
            s["vxp"].inc(ins, 1)
        if c.stage < 2:
            return

        v.wait_ge(s["tqkv"].h, NTT)
        nq = 0

        def _pass1(k):
            nonlocal nq
            for _ in range(k):
                if nq < 128:
                    v.wait_ge(c.dwp[nq % 4].h, 16 * (nq // 4 + 1))
                    q, hf = divmod(nq, 2)
                    m1 = c.m1g if q < 32 else c.m1u
                    qq = q % 32
                    off = (qq % 4) * 704 + hf * 352
                    dst = m1[:, off:off + 352]
                    wp = c.wp1[nq % 4]
                    if qq // 4 == 0:
                        v.tensor_copy(dst, wp[:, :])
                        v.tensor_scalar_mul(wp[:, :], wp[:, :], -1.0)
                        ins = v.tensor_max(dst, dst, wp[:, :])
                    else:
                        v.tensor_max(dst, dst, wp[:, :])
                        v.tensor_scalar_mul(wp[:, :], wp[:, :], -1.0)
                        ins = v.tensor_max(dst, dst, wp[:, :])
                    s["vwp"].inc(ins, 1)
                    nq += 1

        def _epilogue_v(jt):
            sl2 = slice(jt * 256, (jt + 1) * 256)
            v.wait_ge(s["tst2a"].h, jt + 1)
            ins = v.tensor_scalar_mul(c.mu2_row[0:1, sl2],
                                      c.pb[7][0:1, 256:512], 1.0 / D)
            s["vst2"].inc(ins, 1)
            v.wait_ge(s["tst2b"].h, jt + 1)
            v.tensor_scalar_mul(c.var2_row[0:1, sl2], c.pb[7][0:1, 0:256], 1.0 / D)
            v.tensor_mul(c.trow_s[0:1, :], c.mu2_row[0:1, sl2], c.mu2_row[0:1, sl2])
            ins = v.tensor_sub(c.var2_row[0:1, sl2], c.var2_row[0:1, sl2],
                               c.trow_s[0:1, :])
            s["vst2"].inc(ins, 1)
            v.wait_ge(s["tmt"].h, 2 * jt + 1)
            for q in range(2):
                v.reduce_max(c.smalls[:, c.SM_MC + q:c.SM_MC + q + 1],
                             c.pb[7][:, q * 128:(q + 1) * 128], axis=AX.X)
            ins = None
            for q in range(2):
                ins = v.tensor_reduce(c.smalls[:, c.SM_MC + 2 + q:c.SM_MC + 3 + q],
                                      c.pb[7][:, 256 + q * 128:256 + (q + 1) * 128],
                                      axis=AX.X, op=ALU.min)
            s["vmc"].inc(ins, 1)
            v.wait_ge(s["tmt"].h, 2 * jt + 2)
            v.tensor_copy(c.Mrow_s[0:1, :], c.pb[7][0:1, 0:256])
            v.tensor_copy(c.mrow_s[0:1, :], c.pb[7][0:1, 256:512])
            v.tensor_sub(c.Mrow_s[0:1, :], c.Mrow_s[0:1, :], c.mu2_row[0:1, sl2])
            v.tensor_sub(c.mrow_s[0:1, :], c.mu2_row[0:1, sl2], c.mrow_s[0:1, :])
            ins = v.tensor_max(c.m_row[0:1, sl2], c.Mrow_s[0:1, :], c.mrow_s[0:1, :])
            s["vmrow"].inc(ins, 1)

        for jt in range(NJT):
            if jt >= 1:
                _epilogue_v(jt - 1)
            v.wait_ge(s["trr"].h, jt + 1)
            ins = v.reciprocal(c.smalls[:, 14:16], c.pb[6][:, 0:2])
            s["vrv"].inc(ins, 1)
            v.wait_ge(s["trt"].h, jt + 1)
            ins = v.tensor_copy(c.rinv_row[0:1, 0:256], c.pb[7][0:1, 0:256])
            s["vrt"].inc(ins, 1)
            v.wait_ge(s["tb7"].h, jt + 1)
            ins = v.tensor_copy(c.rinv_bc[:, :], c.pb[7][:, 0:256])
            s["vb7"].inc(ins, 1)
            v.wait_ge(s["tAV"].h, (jt + 1) * NIC)
            if jt >= 1:
                v.wait_ge(s["dx2"].h, 16 * DC * jt)
                v.wait_ge(s["sx2b"].h, jt)
            v.wait_ge(c.dxr[jt % 2].h, 16 * DC * (jt // 2 + 1))
            for ec in range(DC):
                sl = slice(ec * 256, (ec + 1) * 256)
                v.tensor_mul(c.x2stage[:, sl],
                             c.pb[2 + ec // 2][:, (ec % 2) * 256:(ec % 2) * 256 + 256],
                             c.rinv_bc[:, :])
                ins = v.tensor_add(c.x2stage[:, sl], c.x2stage[:, sl],
                                   c.xresb[:, sl])
                if ec == DC - 1:
                    s["vOev"].inc(ins, 1)
            v.memset(c.Mt[:, :], -1e30)
            v.memset(c.mt[:, :], 1e30)
            ins = None
            for ec in range(DC):
                sl = slice(ec * 256, (ec + 1) * 256)
                v.tensor_max(c.Mt[:, :], c.Mt[:, :], c.x2stage[:, sl])
                ins = v.tensor_tensor(c.mt[:, :], c.mt[:, :], c.x2stage[:, sl],
                                      ALU.min)
            s["vmt"].inc(ins, 1)
            _pass1(16)
        _epilogue_v(NJT - 1)
        if c.stage < 3:
            return

        c.sched.replay("vector", v, seg="pre")
        c.sched.replay("vector", v, seg="pre2")

        def _wd1(icc):
            v.wait_ge(c.dwd1[icc % 2].h, 16 * (icc // 2 + 1))
            wb = c.wdbuf[icc % 2]
            if icc == 0:
                v.tensor_copy(c.m1d[:, :], wb[:, :])
                v.tensor_scalar_mul(wb[:, :], wb[:, :], -1.0)
                ins = v.tensor_max(c.m1d[:, :], c.m1d[:, :], wb[:, :])
            else:
                v.tensor_max(c.m1d[:, :], c.m1d[:, :], wb[:, :])
                v.tensor_scalar_mul(wb[:, :], wb[:, :], -1.0)
                ins = v.tensor_max(c.m1d[:, :], c.m1d[:, :], wb[:, :])
            s["vwd1"].inc(ins, 1)

        for icc in range(NICC):
            for t4 in range(4):
                it = icc * 4 + t4
                v.wait_ge(s["sgu"].h, (it + 1) * 2)
                if it >= 2:
                    v.wait_ge(c.dhsp[it % 2].h, 16 * (it // 2))
                ht = c.htile[it % 2]
                ins = v.tensor_mul(ht[:, :], c.sg[:, :], c.su[:, :])
                s["vh"].inc(ins, 1)
                ins = v.reduce_max(c.hmcols[:, it:it + 1], ht[:, :], axis=AX.X,
                                   apply_absolute_value=True)
                if it == NICC * 4 - 1:
                    s["vh"].inc(ins, 1)
                if it % 4 == 2:
                    _wd1(it // 4)
        v.wait_ge(s["vh"].h, NICC * 4 + 1)
        ins = v.reduce_max(c.hmax_col[:, :], c.hmcols[:, 0:NICC * 4], axis=AX.X)
        s["vh"].inc(ins, 1)
        c.sched.replay("vector", v, seg="post")
        c.sched.replay("vector", v, seg="post2")
        v.wait_ge(c.chain["vector"].h, 0)
        for hc in range(DC):
            v.wait_ge(c.dx2b[hc % 2].h, 16 * (hc // 2 + 1))
            for t4 in range(4):
                v.wait_ge(s["sdn"].h, hc * 4 + t4 + 1)
                sl = slice(t4 * 512, (t4 + 1) * 512)
                ins = v.tensor_add(c.ostage[hc % 2][:, sl],
                                   c.ostage[hc % 2][:, sl],
                                   c.x2re2[hc % 2][:, sl])
                s["vdn"].inc(ins, 1)



_PROG_CACHE = {}


def _get_program():
    if "full" not in _PROG_CACHE:
        _PROG_CACHE["full"] = build_program(stage=3, taps=False)
    return _PROG_CACHE["full"]


def make_in_maps(x, wq, wk, wv, wgT, wuT, wdT, clip):
    in_maps = []
    for cid in range(NCORES):
        b, jh = cid >> 1, cid & 1
        xb = x[b]
        xr = xb if jh == 0 else np.concatenate([xb[TJ:], xb[:TJ]], axis=0)
        in_maps.append({
            "xT": np.ascontiguousarray(xr.T),
            "wq": wq, "wk": wk, "wv": wv,
            "wgT": wgT, "wuT": wuT, "wdT": wdT,
            "clip": clip,
        })
    return in_maps


def kernel(L0_attn_x, L0_attn_W_query, L0_attn_W_key, L0_attn_W_value,
           L0_ffn_clip_x, L0_ffn_clip_down, L0_ffn_w_gate, L0_ffn_w_up,
           L0_ffn_w_down):
    x = np.ascontiguousarray(np.asarray(L0_attn_x, dtype=np.float32))
    wq = np.ascontiguousarray(np.asarray(L0_attn_W_query, dtype=np.float32))
    wk = np.ascontiguousarray(np.asarray(L0_attn_W_key, dtype=np.float32))
    wv = np.ascontiguousarray(np.asarray(L0_attn_W_value, dtype=np.float32))
    wgT = np.ascontiguousarray(np.asarray(L0_ffn_w_gate, dtype=np.float32).T)
    wuT = np.ascontiguousarray(np.asarray(L0_ffn_w_up, dtype=np.float32).T)
    wdT = np.ascontiguousarray(np.asarray(L0_ffn_w_down, dtype=np.float32).T)
    clip = np.array([[np.asarray(L0_ffn_clip_x).reshape(-1)[0],
                      np.asarray(L0_ffn_clip_down).reshape(-1)[0]]],
                    dtype=np.float32)
    nc = _get_program()
    res = run_bass_kernel_spmd(nc, make_in_maps(x, wq, wk, wv, wgT, wuT, wdT, clip),
                               core_ids=list(range(NCORES)))
    out = np.empty((4, N, D), dtype=np.float32)
    for cid, r in enumerate(res.results):
        b, jh = cid >> 1, cid & 1
        out[b, jh * TJ:(jh + 1) * TJ] = r["outT"].T
    return out
